# version 25
# baseline (speedup 1.0000x reference)
import math
import numpy as np
from contextlib import ExitStack

import concourse.bass as bass
import concourse.mybir as mybir
from concourse.bass_utils import run_bass_kernel_spmd

F32 = mybir.dt.float32
BF16 = mybir.dt.bfloat16
AF = mybir.ActivationFunctionType
ALU = mybir.AluOpType
AX = mybir.AxisListType

B, S, D = 4, 4096, 1024
H, DH, DR, DLAT = 16, 64, 32, 512
NT = S // 128
OWN = 16
TOK = OWN * 128
SCALE = 1.0 / float(np.float32(np.sqrt(np.float32(96.0))))
EHI = float(math.exp(80.0 * SCALE))
ELO = float(math.exp(-80.0 * SCALE))
EPS = 1e-5
BIG = 1e9

ENGINES = ("sync", "tensor", "vector", "scalar", "gpsimd")

_DEBUG_TAPS = []


class Chan:
    def __init__(self, es, nc, name):
        self.sem = es.enter_context(nc.semaphore(name))
        self.n = 0


class Plan:
    def __init__(self):
        self.ops = {e: [] for e in ENGINES}

    def emit(self, engine, fn, waits=(), inc=None, amt=1):
        if inc is not None and amt == 16:
            waits = tuple(waits) + ((inc, inc.n),)
        waits = tuple(waits)

        def do(eng, fn=fn, waits=waits, inc=inc, amt=amt):
            for ch, v in waits:
                if v > 0:
                    eng.wait_ge(ch.sem, v)
            instr = fn(eng)
            if inc is not None:
                instr.then_inc(inc.sem, amt)
        self.ops[engine].append(do)
        if inc is not None:
            inc.n += amt

    def barrier_wait(self, engine, chans):
        snap = [(ch, ch.n) for ch in chans]

        def do(eng, snap=snap):
            for ch, v in snap:
                if v > 0:
                    eng.wait_ge(ch.sem, v)
        self.ops[engine].append(do)


def build_program():
    nc = bass.Bass("TRN2", target_bir_lowering=False, debug=False)

    d_x = nc.dram_tensor("x_seq", [S, D], F32, kind="ExternalInput").ap()
    d_xo = nc.dram_tensor("x_own", [TOK, D], F32, kind="ExternalInput").ap()
    d_masks = nc.dram_tensor("d_masks", [128, 1024], BF16, kind="ExternalInput").ap()
    d_rkc = nc.dram_tensor("ropek_cos", [128, NT * 16], F32, kind="ExternalInput").ap()
    d_rks = nc.dram_tensor("ropek_sin", [128, NT * 16], F32, kind="ExternalInput").ap()
    d_rqc = nc.dram_tensor("ropeq_cos", [128, OWN * 256], F32, kind="ExternalInput").ap()
    d_rqs = nc.dram_tensor("ropeq_sin", [128, OWN * 256], F32, kind="ExternalInput").ap()
    d_idb = nc.dram_tensor("d_ident_bf", [128, 128], BF16, kind="ExternalInput").ap()
    d_idf = nc.dram_tensor("d_ident_f32", [128, 128], F32, kind="ExternalInput").ap()
    d_ones64 = nc.dram_tensor("d_ones64", [1, 64], BF16, kind="ExternalInput").ap()
    d_sel8 = nc.dram_tensor("d_sel8", [8, 1024], BF16, kind="ExternalInput").ap()
    d_wdq = nc.dram_tensor("wdqT", [128, 8 * 512], F32, kind="ExternalInput").ap()
    d_wdkv = nc.dram_tensor("wdkvT", [128, 8 * 512], F32, kind="ExternalInput").ap()
    d_wuq = nc.dram_tensor("wuqT", [128, 4 * 1024], F32, kind="ExternalInput").ap()
    d_wqr = nc.dram_tensor("wqrT", [128, 4 * 512], F32, kind="ExternalInput").ap()
    d_wuk = nc.dram_tensor("wukT", [128, 4 * 1024], F32, kind="ExternalInput").ap()
    d_wuv = nc.dram_tensor("wuvT", [128, 4 * 1024], F32, kind="ExternalInput").ap()
    d_wkr = nc.dram_tensor("wkrT", [128, 8 * 32], F32, kind="ExternalInput").ap()
    d_wo = nc.dram_tensor("woT", [128, 8 * 1024], F32, kind="ExternalInput").ap()
    d_wg = nc.dram_tensor("d_wg", [128, 8 * 8], F32, kind="ExternalInput").ap()
    d_ws1 = nc.dram_tensor("ws1", [16, 128, 1024], F32, kind="ExternalInput").ap()
    d_ws3 = nc.dram_tensor("ws3", [16, 128, 1024], F32, kind="ExternalInput").ap()
    d_ws2 = nc.dram_tensor("ws2", [2, 128, 8 * 1024], F32, kind="ExternalInput").ap()
    d_we1 = nc.dram_tensor("we1", [8, 16, 128, 1024], F32, kind="ExternalInput").ap()
    d_we3 = nc.dram_tensor("we3", [8, 16, 128, 1024], F32, kind="ExternalInput").ap()
    d_we2 = nc.dram_tensor("we2", [8, 2, 128, 8 * 1024], F32, kind="ExternalInput").ap()
    d_out = nc.dram_tensor("out", [TOK, D], F32, kind="ExternalOutput").ap()
    d_xn_sav = nc.dram_tensor("xn_sav", [128, 8 * TOK], BF16, kind="Internal").ap()

    _TAP_DEFS = {
        "t_xn_own": ([128, 8 * TOK], BF16),
        "t_gates": ([128, 128], F32),
        "t_wt": ([8, TOK], BF16),
        "t_ckv": ([128, 4 * S], BF16),
        "t_krf": ([32, S], BF16),
        "t_q": ([96, 16 * TOK], BF16),
        "t_attn": ([128, 8 * TOK], BF16),
        "t_wo": ([128, 8 * TOK], F32),
        "t_acc": ([128, 8 * TOK], F32),
        "t_acc_sh": ([128, 8 * TOK], F32),
        "t_h": ([128, 8 * TOK], BF16),
        "t_bc": ([128, TOK], BF16),
    }
    taps = {}
    for name in _DEBUG_TAPS:
        shape, dt = _TAP_DEFS[name]
        taps[name] = nc.dram_tensor(name, shape, dt, kind="ExternalOutput").ap()

    es_root = ExitStack()
    plan = Plan()

    def sb(name, shape, dt, off):
        return nc.alloc_sbuf_tensor_at(name, shape, dt, offset=off)

    def pst(name, shape, dt, stack):
        return stack.enter_context(nc.psum_tensor(name, shape, dt))

    cS = Chan(es_root, nc, "cS")
    cG = Chan(es_root, nc, "cG")
    cP = Chan(es_root, nc, "cP")
    cV = Chan(es_root, nc, "cV")
    cA = Chan(es_root, nc, "cA")
    ALL_CH = (cS, cG, cP, cV, cA)

    def w(ch):
        return (ch, ch.n)

    def wv(ch, val):
        return (ch, val)

    def barrier():
        for e in ENGINES:
            plan.barrier_wait(e, ALL_CH)

    def maybe_tap(name, src_ap):
        if name not in taps:
            return
        plan.barrier_wait("sync", ALL_CH)
        plan.emit("sync", lambda e, src_ap=src_ap, name=name: e.dma_start(
            out=taps[name], in_=src_ap), inc=cS, amt=16)
        barrier()

    BASE = 16512
    xn_own = sb("xn_own", [128, 8 * TOK], BF16, BASE + 0)
    attn = sb("attn", [128, 8 * TOK], BF16, BASE + 0)
    ident_bf = sb("ident_bf", [128, 128], BF16, BASE + 32768)
    ident_f32 = sb("ident_f32", [128, 128], F32, BASE + 33024)
    ones64 = sb("ones64", [1, 64], BF16, BASE + 33536)
    wg = sb("wg", [128, 64], F32, BASE + 33664)
    sel8 = sb("sel8", [8, 1024], BF16, BASE + 33920)
    masks = sb("masks", [128, 1024], BF16, BASE + 35968)
    wT = sb("wT", [8, TOK], BF16, BASE + 38016)
    eps_col = sb("eps_col", [128, 1], F32, BASE + 42112)
    stats = sb("stats", [128, 32], F32, BASE + 42176)
    gates = sb("gates", [128, 128], F32, BASE + 42304)
    w_tok = sb("w_tok", [128, 128], F32, BASE + 42816)
    gates_s = w_tok
    g_tmp = sb("g_tmp", [128, 8], F32, BASE + 43328)
    g_tmp2 = sb("g_tmp2", [128, 8], F32, BASE + 43392)
    wuk = sb("wuk", [128, 4096], BF16, BASE + 43904)
    wuv = sb("wuv", [128, 4096], BF16, BASE + 52096)
    wkr = sb("wkr", [128, 256], BF16, BASE + 60288)
    w2h = sb("w2h", [128, 8192], BF16, BASE + 43904)
    wdq = sb("wdq", [128, 4096], BF16, BASE + 60800)
    wdkv = sb("wdkv", [128, 4096], BF16, BASE + 68992)
    wuq = sb("wuq", [128, 4096], BF16, BASE + 77184)
    wqr = sb("wqr", [128, 2048], BF16, BASE + 85376)
    k_buf0 = sb("k_buf0", [96, S], BF16, BASE + 60800)
    k_buf1 = sb("k_buf1", [96, S], BF16, BASE + 68992)
    wo_sb = sb("wo_sb", [128, 8192], BF16, BASE + 60800)
    w1c = [sb(f"w1c{i}", [128, 1024], BF16, BASE + 60800 + 2048 * i) for i in range(2)]
    w3c = [sb(f"w3c{i}", [128, 1024], BF16, BASE + 64896 + 2048 * i) for i in range(2)]
    silu_t = sb("silu_t", [128, 1024], BF16, BASE + 68992)
    bcast = sb("bcast", [128, TOK], BF16, BASE + 71040)
    tmp_f = sb("tmp_f", [128, 512], F32, BASE + 75136)
    xn_fm = sb("xn_fm", [128, 8 * S], BF16, BASE + 89472)
    q_all = sb("q_all", [96, 16 * TOK], BF16, BASE + 89472)
    acc = sb("acc", [128, 8 * TOK], F32, BASE + 89472)
    c_kv = sb("c_kv", [128, 4 * S], BF16, BASE + 155008)
    h_half = sb("h_half", [128, 8 * TOK], BF16, BASE + 155008)
    x_tile = [sb(f"x_tile{i}", [128, D], F32, BASE + 187776 + 4096 * i) for i in range(4)]
    xnf_buf2 = [sb(f"xnf_buf{i}", [128, D], F32, BASE + 155008 + 4096 * i) for i in range(2)]
    xnb_buf = [sb(f"xnb_buf{i}", [128, D], BF16, BASE + 163200 + 2048 * i) for i in range(4)]
    stripe = [sb(f"stripe{i}", [128, 1024], F32, BASE + 171392 + 4096 * i) for i in range(2)]
    stripe2 = stripe
    sq_scr = None
    FB = BASE + 187776
    k_rf = sb("k_rf", [32, S], BF16, FB)
    rkc = sb("rkc", [128, NT * 16], F32, FB + 20608)
    rks = sb("rks", [128, NT * 16], F32, FB + 22656)
    krt_buf2 = [sb(f"krt_buf{i}", [128, 32], BF16, FB + 12288 + 64 * i) for i in range(2)]
    r_t1 = sb("r_t1", [128, 256], F32, FB + 12544)
    r_t2 = sb("r_t2", [128, 256], F32, FB + 13568)
    r_t3 = sb("r_t3", [128, 256], F32, FB + 14592)
    r_t4 = sb("r_t4", [128, 256], F32, FB + 15616)
    c_qh = sb("c_qh", [128, 4 * 1024], BF16, FB + 8192)
    rq_c = [sb(f"rq_c{i}", [128, 256], F32, FB + 16384 + 1024 * i) for i in range(2)]
    rq_s = [sb(f"rq_s{i}", [128, 256], F32, FB + 18432 + 1024 * i) for i in range(2)]
    qr_buf2 = [sb(f"qr_buf{i}", [128, 512], BF16, FB + 20480 + 1024 * i) for i in range(2)]
    q_t1 = sb("q_t1", [128, 256], F32, FB + 22528)
    q_t2 = sb("q_t2", [128, 256], F32, FB + 23552)
    v_aug = [sb(f"v_aug{i}", [128, 65 * NT], BF16, FB + 8192 + 4224 * i) for i in range(2)]
    pt_buf = [sb(f"pt_buf{i}", [128, 1024], BF16, FB + 16640 + 2048 * i) for i in range(3)]
    rsum = sb("rsum", [1, 256], F32, FB + 22784)
    rsum_bf = sb("rsum_bf", [1, 256], BF16, FB + 23808)
    bc_sb = sb("bc_sb", [64, 256], BF16, FB + 24320)
    k_buf = [k_buf0, k_buf1]
    xo_buf = [sb(f"xo_buf{i}", [128, D], F32, FB + 4096 * i) for i in range(4)]
    out_sb = [sb(f"out_sb{i}", [128, D], F32, FB + 16384 + 4096 * i) for i in range(2)]

    plan.emit("sync", lambda e: e.dma_start(out=ident_bf[:], in_=d_idb), inc=cS, amt=16)
    plan.emit("sync", lambda e: e.dma_start(out=ident_f32[:], in_=d_idf), inc=cS, amt=16)
    plan.emit("sync", lambda e: e.dma_start(out=ones64[:], in_=d_ones64), inc=cS, amt=16)
    plan.emit("sync", lambda e: e.dma_start(out=sel8[:], in_=d_sel8), inc=cS, amt=16)
    plan.emit("sync", lambda e: e.dma_start(out=masks[:], in_=d_masks), inc=cS, amt=16)
    plan.emit("sync", lambda e: e.dma_start(out=rkc[:], in_=d_rkc), inc=cS, amt=16)
    plan.emit("sync", lambda e: e.dma_start(out=rks[:], in_=d_rks), inc=cS, amt=16)
    consts_done = cS.n
    plan.emit("vector", lambda e: e.memset(eps_col[:], EPS), inc=cV)

    for dst, src in ((wdq, d_wdq), (wdkv, d_wdkv), (wuq, d_wuq), (wqr, d_wqr),
                     (wuk, d_wuk), (wuv, d_wuv), (wkr, d_wkr)):
        plan.emit("gpsimd", lambda e, dst=dst, src=src: e.dma_start(out=dst[:], in_=src),
                  inc=cG, amt=16)
    plan.emit("gpsimd", lambda e: e.dma_start(out=wg[:], in_=d_wg), inc=cG, amt=16)
    attn_w_done = cG.n

    es_ps0 = ExitStack()
    ps_tr = [pst(f"ps_tr{i}", [128, 1024], BF16, es_ps0) for i in range(4)]
    ps_trf = [pst("ps_trf0", [128, 1024], F32, es_ps0)]
    ps_gate = pst("ps_gate", [128, 512], F32, es_ps0)
    ps_wt = pst("ps_wt", [128, 512], F32, es_ps0)

    x_free = [[], [], [], []]
    xnb_free = [(cP, 0)] * 4
    xnf_free = [(cP, 0), (cP, 0)]
    stripe_free = [(cP, 0), (cP, 0)]
    pstr_free = [(cA, 0)] * 4
    pstrf_free = [(cA, 0)]
    gate_ps_free = [(cA, 0)]
    xn_tile_done = []

    def p0_stageA(i, src_rows):
        b4 = i % 4
        bi = i % 2
        sc = 8 * b4
        plan.emit("sync", lambda e: e.dma_start(out=x_tile[b4][:], in_=src_rows),
                  waits=list(x_free[b4]), inc=cS, amt=16)
        ld = cS.n
        plan.emit("vector", lambda e: e.reduce_sum(stats[:, sc:sc + 1], x_tile[b4][:],
                                                   axis=AX.X),
                  waits=[wv(cS, ld)], inc=cV)
        plan.emit("scalar", lambda e: e.activation(
            out=stripe[bi][:], in_=x_tile[b4][:], func=AF.Square,
            accum_out=stats[:, sc + 2:sc + 3]),
            waits=[wv(cS, ld), stripe_free[bi]], inc=cA)
        sq[i] = cA.n
        plan.emit("vector", lambda e: e.tensor_scalar(
            out=stats[:, sc + 1:sc + 2], in0=stats[:, sc:sc + 1], scalar1=1.0 / D,
            scalar2=None, op0=ALU.mult), waits=[w(cV)], inc=cV)
        plan.emit("vector", lambda e: e.tensor_mul(
            stats[:, sc + 5:sc + 6], stats[:, sc + 1:sc + 2], stats[:, sc + 1:sc + 2]),
            waits=[w(cV)], inc=cV)

    def p0_stageB(i):
        b4 = i % 4
        sc = 8 * b4
        plan.emit("vector", lambda e: e.tensor_scalar(
            out=stats[:, sc + 3:sc + 4], in0=stats[:, sc + 2:sc + 3], scalar1=1.0 / D,
            scalar2=None, op0=ALU.mult), waits=[w(cV), wv(cA, sq[i])], inc=cV)
        plan.emit("vector", lambda e: e.tensor_sub(
            stats[:, sc + 3:sc + 4], stats[:, sc + 3:sc + 4], stats[:, sc + 5:sc + 6]),
            waits=[w(cV)], inc=cV)
        var_done = cV.n
        plan.emit("scalar", lambda e: e.activation(
            out=stats[:, sc + 4:sc + 5], in_=stats[:, sc + 3:sc + 4], func=AF.Sqrt,
            bias=eps_col[:, :]), waits=[wv(cV, var_done)], inc=cA)
        sqr[i] = cA.n

    def p0_stageC(i, dst_fm, dst_cols_base, own_idx):
        b4 = i % 4
        bi = i % 2
        sc = 8 * b4
        plan.emit("vector", lambda e: e.reciprocal(stats[:, sc + 6:sc + 7],
                                                   stats[:, sc + 4:sc + 5]),
                  waits=[wv(cA, sqr[i])], inc=cV)
        plan.emit("vector", lambda e: e.scalar_tensor_tensor(
            out=stats[:, sc + 7:sc + 8], in0=stats[:, sc + 1:sc + 2], scalar=-1.0,
            in1=stats[:, sc + 6:sc + 7], op0=ALU.mult, op1=ALU.mult),
            waits=[w(cV)], inc=cV)
        plan.emit("vector", lambda e: e.tensor_scalar(
            out=xnb_buf[b4][:], in0=x_tile[b4][:], scalar1=stats[:, sc + 6:sc + 7],
            scalar2=stats[:, sc + 7:sc + 8], op0=ALU.mult, op1=ALU.add),
            waits=[w(cV), xnb_free[b4]], inc=cV)
        xnb_done = cV.n
        xnf_done = 0
        if own_idx is not None:
            plan.emit("vector", lambda e: e.tensor_scalar(
                out=xnf_buf2[bi][:], in0=x_tile[b4][:], scalar1=stats[:, sc + 6:sc + 7],
                scalar2=stats[:, sc + 7:sc + 8], op0=ALU.mult, op1=ALU.add),
                waits=[w(cV), xnf_free[bi]], inc=cV)
            xnf_done = cV.n
        x_free[b4] = [wv(cV, cV.n), wv(cA, sq[i])]
        for kk in range(8):
            plan.emit("tensor", lambda e, kk=kk: e.transpose(
                ps_tr[b4][:, 128 * kk:128 * kk + 128],
                xnb_buf[b4][:, 128 * kk:128 * kk + 128], ident_bf[:]),
                waits=([wv(cV, xnb_done), pstr_free[b4], wv(cS, consts_done)]
                       if kk == 0 else []),
                inc=(cP if kk == 7 else None))
        xnb_free[b4] = w(cP)
        tr_done = cP.n
        dst_view = dst_fm[:].rearrange("p (k n) -> p k n", k=8)
        plan.emit("scalar", lambda e, dst_view=dst_view: e.copy(
            out=dst_view[:, :, dst_cols_base:dst_cols_base + 128],
            in_=ps_tr[b4][:].rearrange("p (k n) -> p k n", k=8)),
            waits=[wv(cP, tr_done)], inc=cA)
        pstr_free[b4] = w(cA)
        xn_tile_done.append(cA.n)
        if own_idx is None:
            return
        o = own_idx
        for kk in range(8):
            plan.emit("tensor", lambda e, kk=kk: e.transpose(
                ps_trf[0][:, 128 * kk:128 * kk + 128],
                xnf_buf2[bi][:, 128 * kk:128 * kk + 128], ident_f32[:]),
                waits=([wv(cV, xnf_done), pstrf_free[0]] if kk == 0 else []),
                inc=(cP if kk == 7 else None))
        xnf_free[bi] = w(cP)
        trf_done = cP.n
        sq_guard = sq.get(o + NT + 2, sq[o + NT])
        plan.emit("scalar", lambda e: e.copy(out=stripe2[bi][:], in_=ps_trf[0][:]),
                  waits=[wv(cP, trf_done), wv(cA, sq_guard)], inc=cA)
        pstrf_free[0] = w(cA)
        st_done = cA.n
        for kk in range(8):
            plan.emit("tensor", lambda e, kk=kk: e.matmul(
                ps_gate[0:128, 0:8], stripe2[bi][:, 128 * kk:128 * kk + 128],
                wg[:, 8 * kk:8 * kk + 8], start=(kk == 0), stop=(kk == 7)),
                waits=([wv(cA, st_done), gate_ps_free[0], wv(cG, attn_w_done)]
                       if kk == 0 else []),
                inc=(cP if kk == 7 else None))
        stripe_free[bi] = w(cP)
        gmm_done = cP.n
        plan.emit("scalar", lambda e, o=o: e.copy(
            out=gates[:, 8 * o:8 * o + 8], in_=ps_gate[0:128, 0:8]),
            waits=[wv(cP, gmm_done)], inc=cA)
        gate_ps_free[0] = w(cA)

    sq = {}
    sqr = {}
    tiles = ([(t, d_x[128 * t:128 * t + 128, :], xn_fm, 128 * t, None) for t in range(NT)] +
             [(NT + o, d_xo[128 * o:128 * o + 128, :], xn_own, 128 * o, o)
              for o in range(OWN)])
    for i in range(len(tiles) + 4):
        if 4 <= i and (i - 4) < len(tiles):
            tj = tiles[i - 4]
            p0_stageC(tj[0], tj[2], tj[3], tj[4])
        if 2 <= i and (i - 2) < len(tiles):
            p0_stageB(tiles[i - 2][0])
        if i < len(tiles):
            ti = tiles[i]
            p0_stageA(ti[0], ti[1])
    p0_done_A = cA.n

    maybe_tap("t_xn_own", xn_own[:])
    plan.emit("sync", lambda e: e.dma_start(out=d_xn_sav, in_=xn_own[:]),
              waits=[wv(cA, p0_done_A)], inc=cS, amt=16)
    xn_saved = cS.n

    plan.emit("scalar", lambda e: e.activation(out=gates_s[:], in_=gates[:], func=AF.Sigmoid),
              waits=[wv(cA, p0_done_A)], inc=cA)
    sig_done = cA.n
    wt_ps_free = [(cA, 0)]
    for o in range(OWN):
        gsl = gates_s[:, 8 * o:8 * o + 8]
        plan.emit("vector", lambda e, gsl=gsl: e.reduce_max(stats[:, 0:1], gsl, axis=AX.X),
                  waits=[wv(cA, sig_done), w(cV)], inc=cV)
        plan.emit("vector", lambda e, gsl=gsl: e.tensor_scalar(
            out=g_tmp[:], in0=gsl, scalar1=stats[:, 0:1], scalar2=None, op0=ALU.is_ge),
            waits=[w(cV)], inc=cV)
        plan.emit("vector", lambda e: e.tensor_scalar(
            out=g_tmp[:], in0=g_tmp[:], scalar1=BIG, scalar2=None, op0=ALU.mult),
            waits=[w(cV)], inc=cV)
        plan.emit("vector", lambda e, gsl=gsl: e.tensor_sub(g_tmp2[:], gsl, g_tmp[:]),
                  waits=[w(cV)], inc=cV)
        plan.emit("vector", lambda e: e.reduce_max(stats[:, 1:2], g_tmp2[:], axis=AX.X),
                  waits=[w(cV)], inc=cV)
        plan.emit("vector", lambda e, gsl=gsl: e.tensor_scalar(
            out=g_tmp[:], in0=gsl, scalar1=stats[:, 1:2], scalar2=None, op0=ALU.is_ge),
            waits=[w(cV)], inc=cV)
        plan.emit("vector", lambda e, gsl=gsl: e.tensor_mul(g_tmp[:], g_tmp[:], gsl),
                  waits=[w(cV)], inc=cV)
        plan.emit("vector", lambda e: e.tensor_scalar(
            out=stats[:, 2:3], in0=stats[:, 0:1], scalar1=stats[:, 1:2], scalar2=3.0,
            op0=ALU.add, op1=ALU.mult), waits=[w(cV)], inc=cV)
        plan.emit("vector", lambda e: e.reciprocal(stats[:, 3:4], stats[:, 2:3]),
                  waits=[w(cV)], inc=cV)
        plan.emit("vector", lambda e, o=o: e.tensor_scalar(
            out=w_tok[:, 8 * o:8 * o + 8], in0=g_tmp[:], scalar1=stats[:, 3:4], scalar2=None,
            op0=ALU.mult), waits=[w(cV)], inc=cV)
        wt_done = cV.n
        plan.emit("tensor", lambda e, o=o: e.transpose(
            ps_wt[0:8, 0:128], w_tok[:, 8 * o:8 * o + 8], ident_f32[:]),
            waits=[wv(cV, wt_done), wt_ps_free[0]], inc=cP)
        tw_done = cP.n
        plan.emit("scalar", lambda e, o=o: e.copy(
            out=wT[0:8, 128 * o:128 * o + 128], in_=ps_wt[0:8, 0:128]),
            waits=[wv(cP, tw_done)], inc=cA)
        wt_ps_free[0] = w(cA)
    es_ps0.close()
    maybe_tap("t_gates", gates_s[:])
    maybe_tap("t_wt", wT[:])

    es_ps1 = ExitStack()
    ps_m = [pst(f"ps_m{i}", [128, 512], F32, es_ps1) for i in range(4)]
    psm_free = [(cA, 0)] * 4
    xn_fm_v = xn_fm[:].rearrange("p (k n) -> p k n", k=8)
    ckv_v = c_kv[:].rearrange("p (m n) -> p m n", m=4)

    for nt in range(8):
        for m in range(4):
            bi = m
            for kk in range(8):
                plan.emit("tensor", lambda e, nt=nt, m=m, bi=bi, kk=kk: e.matmul(
                    ps_m[bi][:, 0:512],
                    wdkv[:, 512 * kk + 128 * m:512 * kk + 128 * m + 128],
                    xn_fm_v[:, kk, 512 * nt:512 * nt + 512],
                    start=(kk == 0), stop=(kk == 7)),
                    waits=([wv(cG, attn_w_done), wv(cA, p0_done_A), psm_free[bi]]
                           if kk == 0 else []),
                    inc=(cP if kk == 7 else None))
            mm_done = cP.n
            plan.emit("scalar", lambda e, nt=nt, m=m, bi=bi: e.copy(
                out=ckv_v[:, m, 512 * nt:512 * nt + 512], in_=ps_m[bi][:, 0:512]),
                waits=[wv(cP, mm_done)], inc=cA)
            psm_free[bi] = w(cA)

    es_kr = ExitStack()
    ps_kr = [pst(f"ps_kr{i}", [128, 512], F32, es_kr) for i in range(2)]
    ps_krt = pst("ps_krt", [128, 1024], BF16, es_kr)
    pskr_free = [(cV, 0), (cV, 0)]
    krt_free = [(cP, 0), (cP, 0)]
    pskrt_free = [(cA, 0)]
    for t in range(NT):
        kbi = t % 2
        for kk in range(8):
            plan.emit("tensor", lambda e, t=t, kk=kk, kbi=kbi: e.matmul(
                ps_kr[kbi][:, 0:32], xn_fm_v[:, kk, 128 * t:128 * t + 128],
                wkr[:, 32 * kk:32 * kk + 32], start=(kk == 0), stop=(kk == 7)),
                waits=([pskr_free[kbi]] if kk == 0 else []), inc=(cP if kk == 7 else None))
        kr_done = cP.n
        cosv = rkc[:, 16 * t:16 * t + 16]
        sinv = rks[:, 16 * t:16 * t + 16]
        x1 = ps_kr[kbi][:, 0:32].rearrange("p (n two) -> p n two", two=2)[:, :, 0]
        x2 = ps_kr[kbi][:, 0:32].rearrange("p (n two) -> p n two", two=2)[:, :, 1]
        oe = krt_buf2[kbi][:, 0:32].rearrange("p (n two) -> p n two", two=2)[:, :, 0]
        oo = krt_buf2[kbi][:, 0:32].rearrange("p (n two) -> p n two", two=2)[:, :, 1]
        plan.emit("vector", lambda e, x1=x1, cosv=cosv: e.tensor_mul(r_t1[:, 0:16], x1, cosv),
                  waits=[wv(cP, kr_done), wv(cS, consts_done), krt_free[kbi]], inc=cV)
        plan.emit("vector", lambda e, x2=x2, sinv=sinv: e.tensor_mul(r_t2[:, 0:16], x2, sinv),
                  inc=cV)
        plan.emit("vector", lambda e, x1=x1, sinv=sinv: e.tensor_mul(r_t3[:, 0:16], x1, sinv),
                  inc=cV)
        plan.emit("vector", lambda e, x2=x2, cosv=cosv: e.tensor_mul(r_t4[:, 0:16], x2, cosv),
                  inc=cV)
        plan.emit("vector", lambda e, oe=oe: e.tensor_sub(oe, r_t1[:, 0:16], r_t2[:, 0:16]),
                  waits=[w(cV)], inc=cV)
        plan.emit("vector", lambda e, oo=oo: e.tensor_add(oo, r_t3[:, 0:16], r_t4[:, 0:16]),
                  inc=cV)
        rope_done = cV.n
        pskr_free[kbi] = w(cV)
        plan.emit("tensor", lambda e, t=t, kbi=kbi: e.transpose(
            ps_krt[0:32, 0:128], krt_buf2[kbi][:, 0:32], ident_bf[:]),
            waits=[wv(cV, rope_done), pskrt_free[0]], inc=cP)
        krt_free[kbi] = w(cP)
        tkr_done = cP.n
        plan.emit("scalar", lambda e, t=t: e.copy(
            out=k_rf[0:32, 128 * t:128 * t + 128], in_=ps_krt[0:32, 0:128]),
            waits=[wv(cP, tkr_done)], inc=cA)
        pskrt_free[0] = w(cA)
    kr_v_all = cV.n
    es_kr.close()
    maybe_tap("t_ckv", c_kv[:])
    maybe_tap("t_krf", k_rf[:])

    xo_v = xn_own[:].rearrange("p (k n) -> p k n", k=8)
    cq_v = c_qh[:].rearrange("p (m n) -> p m n", m=4)
    es_qr = ExitStack()
    ps_qr = [pst(f"ps_qr{i}", [128, 512], F32, es_qr) for i in range(2)]
    ps_qrt = [pst(f"ps_qrt{i}", [128, 1024], BF16, es_qr) for i in range(2)]
    psqr_free = [(cV, 0), (cV, 0)]
    qrb_free = [(cP, 0), (cP, 0)]
    psqrt_free = [(cV, 0), (cV, 0)]
    rq_free = [(cV, 0), (cV, 0)]
    cq_free = [(cP, 0)]
    cq_done = {}
    for half in range(2):
        tb = 1024 * half
        for nt in range(2):
            for m in range(4):
                bi = m
                for kk in range(8):
                    plan.emit("tensor", lambda e, nt=nt, m=m, bi=bi, kk=kk, tb=tb: e.matmul(
                        ps_m[bi][:, 0:512],
                        wdq[:, 512 * kk + 128 * m:512 * kk + 128 * m + 128],
                        xo_v[:, kk, tb + 512 * nt:tb + 512 * nt + 512],
                        start=(kk == 0), stop=(kk == 7)),
                        waits=([psm_free[bi]] + ([cq_free[0]] if (nt == 0 and kk == 0) else [])
                               if kk == 0 else []),
                        inc=(cP if kk == 7 else None))
                mm_done = cP.n
                plan.emit("scalar", lambda e, nt=nt, m=m, bi=bi: e.copy(
                    out=cq_v[:, m, 512 * nt:512 * nt + 512], in_=ps_m[bi][:, 0:512]),
                    waits=[wv(cP, mm_done)], inc=cA)
                psm_free[bi] = w(cA)
            cq_done[nt] = cA.n
        for hp in range(8):
            for nt in range(2):
                bi = nt
                for kk in range(4):
                    plan.emit("tensor", lambda e, hp=hp, nt=nt, bi=bi, kk=kk: e.matmul(
                        ps_m[bi][:, 0:512],
                        wuq[:, 1024 * kk + 128 * hp:1024 * kk + 128 * hp + 128],
                        cq_v[:, kk, 512 * nt:512 * nt + 512],
                        start=(kk == 0), stop=(kk == 3)),
                        waits=([psm_free[bi], wv(cA, cq_done[nt])] if kk == 0 else []),
                        inc=(cP if kk == 3 else None))
                mm_done = cP.n
                plan.emit("scalar", lambda e, hp=hp, nt=nt, bi=bi, tb=tb: e.copy(
                    out=q_all[0:64, TOK * (2 * hp) + tb + 512 * nt:
                              TOK * (2 * hp) + tb + 512 * nt + 512],
                    in_=ps_m[bi][0:64, 0:512]), waits=[wv(cP, mm_done)], inc=cA)
                plan.emit("scalar", lambda e, hp=hp, nt=nt, bi=bi, tb=tb: e.copy(
                    out=q_all[0:64, TOK * (2 * hp + 1) + tb + 512 * nt:
                              TOK * (2 * hp + 1) + tb + 512 * nt + 512],
                    in_=ps_m[bi][64:128, 0:512]), inc=cA)
                psm_free[bi] = w(cA)
        for oo in range(8):
            o = 8 * half + oo
            bi = o % 2
            plan.emit("sync", lambda e, o=o, bi=bi: e.dma_start(
                out=rq_c[bi][:], in_=d_rqc[:, 256 * o:256 * o + 256]),
                waits=[rq_free[bi]] + ([wv(cV, kr_v_all)] if o < 2 else []),
                inc=cS, amt=16)
            plan.emit("sync", lambda e, o=o, bi=bi: e.dma_start(
                out=rq_s[bi][:], in_=d_rqs[:, 256 * o:256 * o + 256]), inc=cS, amt=16)
            rq_ld = cS.n
            qbi = o % 2
            for kk in range(4):
                plan.emit("tensor", lambda e, oo=oo, kk=kk, qbi=qbi: e.matmul(
                    ps_qr[qbi][:, 0:512], cq_v[:, kk, 128 * oo:128 * oo + 128],
                    wqr[:, 512 * kk:512 * kk + 512], start=(kk == 0), stop=(kk == 3)),
                    waits=([psqr_free[qbi], wv(cA, cq_done[oo // 4])] if kk == 0 else []),
                    inc=(cP if kk == 3 else None))
            qr_done = cP.n
            x1 = ps_qr[qbi][:, 0:512].rearrange("p (n two) -> p n two", two=2)[:, :, 0]
            x2 = ps_qr[qbi][:, 0:512].rearrange("p (n two) -> p n two", two=2)[:, :, 1]
            oe = qr_buf2[qbi][:, 0:512].rearrange("p (n two) -> p n two", two=2)[:, :, 0]
            oo_v = qr_buf2[qbi][:, 0:512].rearrange("p (n two) -> p n two", two=2)[:, :, 1]
            plan.emit("vector", lambda e, x1=x1, bi=bi: e.tensor_mul(q_t1[:], x1, rq_c[bi][:]),
                      waits=[wv(cP, qr_done), wv(cS, rq_ld), qrb_free[qbi]], inc=cV)
            plan.emit("vector", lambda e, x2=x2, bi=bi: e.tensor_mul(q_t2[:], x2, rq_s[bi][:]),
                      inc=cV)
            plan.emit("vector", lambda e, oe=oe: e.tensor_sub(oe, q_t1[:], q_t2[:]),
                      waits=[w(cV)], inc=cV)
            plan.emit("vector", lambda e, x1=x1, bi=bi: e.tensor_mul(q_t1[:], x1, rq_s[bi][:]),
                      waits=[w(cV)], inc=cV)
            plan.emit("vector", lambda e, x2=x2, bi=bi: e.tensor_mul(q_t2[:], x2, rq_c[bi][:]),
                      inc=cV)
            plan.emit("vector", lambda e, oo_v=oo_v: e.tensor_add(oo_v, q_t1[:], q_t2[:]),
                      waits=[w(cV)], inc=cV)
            rope_done = cV.n
            rq_free[bi] = w(cV)
            psqr_free[qbi] = w(cV)
            for hh in range(8):
                tb = hh % 2
                plan.emit("tensor", lambda e, hh=hh, qbi=qbi, tb=tb: e.transpose(
                    ps_qrt[tb][0:64, 0:128], qr_buf2[qbi][:, 64 * hh:64 * hh + 64],
                    ident_bf[:]),
                    waits=([wv(cV, rope_done), psqrt_free[tb]] if hh < 2
                           else [psqrt_free[tb]]), inc=cP)
                t_done = cP.n
                plan.emit("vector", lambda e, o=o, hh=hh, tb=tb: e.tensor_copy(
                    q_all[64:96, TOK * (2 * hh) + 128 * o:TOK * (2 * hh) + 128 * o + 128],
                    ps_qrt[tb][0:32, 0:128]), waits=[wv(cP, t_done)], inc=cV)
                plan.emit("vector", lambda e, o=o, hh=hh, tb=tb: e.tensor_copy(
                    q_all[64:96, TOK * (2 * hh + 1) + 128 * o:
                          TOK * (2 * hh + 1) + 128 * o + 128],
                    ps_qrt[tb][32:64, 0:128]), inc=cV)
                psqrt_free[tb] = w(cV)
            qrb_free[qbi] = w(cP)
        cq_free[0] = w(cP)
    es_qr.close()
    es_ps1.close()
    maybe_tap("t_q", q_all[:])

    barrier()

    es_psA = ExitStack()
    ps_s = [pst(f"ps_s{i}", [128, 1024], F32, es_psA) for i in range(2)]
    ps_at2 = [pst(f"ps_at{i}", [128, 512], F32, es_psA) for i in range(2)]
    ps_bc = pst("ps_bc", [128, 512], F32, es_psA)
    ps_kv = [ps_s[0], ps_s[1]]

    kbuf_free = [(cP, 0), (cP, 0)]
    vaug_free = [(cP, 0), (cP, 0)]
    pss_free = [[(cA, 0)], [(cA, 0)]]
    pskv_free = [pss_free[0], pss_free[1]]
    pt_free = [(cP, 0), (cP, 0), (cP, 0)]
    psat_free = [[(cV, 0)], [(cV, 0)]]
    psbc_free = [(cV, 0)]

    def build_pair(hp):
        for nt in range(8):
            bi = nt % 2
            for kk in range(4):
                plan.emit("tensor", lambda e, hp=hp, nt=nt, bi=bi, kk=kk: e.matmul(
                    ps_kv[bi][:, 0:512],
                    wuk[:, 1024 * kk + 128 * hp:1024 * kk + 128 * hp + 128],
                    ckv_v[:, kk, 512 * nt:512 * nt + 512],
                    start=(kk == 0), stop=(kk == 3)),
                    waits=(([pskv_free[bi][0]] +
                            ([kbuf_free[0], kbuf_free[1], wv(cS, xn_saved)]
                             if nt == 0 else [])) if kk == 0 else []),
                    inc=(cP if kk == 3 else None))
            mm_done = cP.n
            plan.emit("vector", lambda e, nt=nt, bi=bi: e.tensor_copy(
                k_buf[0][0:64, 512 * nt:512 * nt + 512], ps_kv[bi][0:64, 0:512]),
                waits=[wv(cP, mm_done)], inc=cV)
            plan.emit("vector", lambda e, nt=nt, bi=bi: e.tensor_copy(
                k_buf[1][0:64, 512 * nt:512 * nt + 512], ps_kv[bi][64:128, 0:512]),
                inc=cV)
            pskv_free[bi][0] = w(cV)
        plan.emit("vector", lambda e: e.tensor_copy(k_buf[0][64:96, :], k_rf[0:32, :]),
                  waits=[kbuf_free[0]], inc=cV)
        plan.emit("vector", lambda e: e.tensor_copy(k_buf[1][64:96, :], k_rf[0:32, :]),
                  waits=[kbuf_free[1]], inc=cV)
        plan.emit("vector", lambda e: e.memset(
            v_aug[0][:].rearrange("p (c ww) -> p c ww", ww=65)[:, :, 64:65], 1.0),
            waits=[vaug_free[0]], inc=cV)
        plan.emit("vector", lambda e: e.memset(
            v_aug[1][:].rearrange("p (c ww) -> p c ww", ww=65)[:, :, 64:65], 1.0),
            waits=[vaug_free[1]], inc=cV)
        for cc in range(NT):
            bi = cc % 2
            for kk in range(4):
                plan.emit("tensor", lambda e, hp=hp, cc=cc, bi=bi, kk=kk: e.matmul(
                    ps_kv[bi][:, 0:128], ckv_v[:, kk, 128 * cc:128 * cc + 128],
                    wuv[:, 1024 * kk + 128 * hp:1024 * kk + 128 * hp + 128],
                    start=(kk == 0), stop=(kk == 3)),
                    waits=([pskv_free[bi][0]] if kk == 0 else []),
                    inc=(cP if kk == 3 else None))
            mm_done = cP.n
            plan.emit("vector", lambda e, cc=cc, bi=bi: e.tensor_copy(
                v_aug[0][:, 65 * cc:65 * cc + 64], ps_kv[bi][:, 0:64]),
                waits=[wv(cP, mm_done)], inc=cV)
            plan.emit("vector", lambda e, cc=cc, bi=bi: e.tensor_copy(
                v_aug[1][:, 65 * cc:65 * cc + 64], ps_kv[bi][:, 64:128]),
                inc=cV)
            pskv_free[bi][0] = w(cV)

    bcmm_done = [0]

    def attn_pair(h0, h1):
        kv_ready = (w(cV), w(cA))
        quads = [(h, g, qd) for h in (h0, h1) for g in range(8) for qd in range(g + 1)]
        pv_q = []
        tail_q = []

        def emit_pv(h, g, qd, par, last_quad, m_done):
            p = h % 2
            vb = v_aug[p]
            at = ps_at2[g % 2]
            for j in range(4):
                cc = 4 * qd + j
                plan.emit("tensor", lambda e, par=par, cc=cc, j=j, vb=vb, qd=qd, at=at,
                          last_quad=last_quad: e.matmul(
                    at[0:65, 0:256], vb[:, 65 * cc:65 * cc + 65],
                    pt_buf[par][:, 256 * j:256 * j + 256],
                    start=(qd == 0 and j == 0), stop=(last_quad and j == 3),
                    skip_group_check=True),
                    waits=([wv(cV, m_done)] + ([psat_free[g % 2][0]]
                                               if (qd == 0 and j == 0)
                                               else [])) if j == 0 else [],
                    inc=(cP if j == 3 else None))
            pt_free[par] = w(cP)
            if last_quad:
                while tail_q:
                    tail_q.pop(0)[1]()
                tail_q.append((cur_i[0], make_tail(h, g)))

        def make_tail(h, g):
            pv_done = cP.n
            at = ps_at2[g % 2]
            plan.emit("scalar", lambda e, at=at: e.activation(
                out=rsum[0:1, 0:256], in_=at[64:65, 0:256], func=AF.Ln),
                waits=[wv(cP, pv_done), w(cA)], inc=cA)
            plan.emit("scalar", lambda e: e.activation(
                out=rsum_bf[0:1, 0:256], in_=rsum[0:1, 0:256], func=AF.Exp, scale=-1.0),
                waits=[w(cA), wv(cP, bcmm_done[0])], inc=cA)
            rs_done = cA.n

            def tail():
                plan.emit("tensor", lambda e: e.matmul(
                    ps_bc[0:64, 0:256], ones64[0:1, 0:64], rsum_bf[0:1, 0:256],
                    start=True, stop=True),
                    waits=[wv(cA, rs_done), psbc_free[0]], inc=cP)
                bcmm_done[0] = cP.n
                bc_done = cP.n
                plan.emit("vector", lambda e: e.tensor_copy(
                    bc_sb[0:64, 0:256], ps_bc[0:64, 0:256]),
                    waits=[wv(cP, bc_done)], inc=cV)
                psbc_free[0] = w(cV)
                bcs_done = cV.n
                plan.emit("vector", lambda e, h=h, g=g, at=at: e.tensor_tensor(
                    out=attn[64 * (h % 2):64 * (h % 2) + 64,
                             TOK * (h // 2) + 256 * g:TOK * (h // 2) + 256 * g + 256],
                    in0=at[0:64, 0:256], in1=bc_sb[0:64, 0:256], op=ALU.mult),
                    waits=[wv(cV, bcs_done), w(cV)], inc=cV)
                psat_free[g % 2][0] = w(cV)
            return tail

        cur_i = [0]
        for i, (h, g, qd) in enumerate(quads):
            cur_i[0] = i
            p = h % 2
            kb = k_buf[p]
            par = i % 2
            for j in range(4):
                cc = 4 * qd + j
                plan.emit("tensor", lambda e, h=h, g=g, cc=cc, j=j, par=par,
                          kb=kb: e.matmul(
                    ps_s[par][:, 256 * j:256 * j + 256],
                    kb[0:96, 128 * cc:128 * cc + 128],
                    q_all[0:96, TOK * h + 256 * g:TOK * h + 256 * g + 256],
                    start=True, stop=True),
                    waits=(([pss_free[par][0]] +
                            (list(kv_ready) if i == 0 else []))
                           if j == 0 else []),
                    inc=(cP if j == 3 else None))
            s_done = cP.n
            if tail_q and (i - tail_q[0][0]) >= 2:
                tail_q.pop(0)[1]()
            if len(pv_q) >= 2:
                emit_pv(*pv_q.pop(0))
            plan.emit("scalar", lambda e, par=par: e.activation(
                out=pt_buf[par][:], in_=ps_s[par][:, 0:1024], func=AF.Exp,
                scale=SCALE),
                waits=[wv(cP, s_done), pt_free[par]], inc=cA)
            pss_free[par][0] = w(cA)
            e_done = cA.n
            last_quad = (qd == g)
            plan.emit("vector", lambda e, par=par: e.tensor_scalar(
                out=pt_buf[par][:], in0=pt_buf[par][:], scalar1=EHI, scalar2=ELO,
                op0=ALU.min, op1=ALU.max),
                waits=[wv(cA, e_done)], inc=cV)
            if last_quad:
                plan.emit("vector", lambda e, par=par: e.tensor_tensor(
                    out=pt_buf[par][:], in0=pt_buf[par][:],
                    in1=masks[:, 0:1024], op=ALU.mult),
                    waits=[w(cV)], inc=cV)
            m_done = cV.n
            pv_q.append((h, g, qd, par, last_quad, m_done))
        while pv_q:
            emit_pv(*pv_q.pop(0))
            if tail_q:
                tail_q.pop(0)[1]()
        while tail_q:
            tail_q.pop(0)[1]()
        for p in (0, 1):
            kbuf_free[p] = w(cP)
            vaug_free[p] = w(cP)

    for hp in range(8):
        build_pair(hp)
        attn_pair(2 * hp, 2 * hp + 1)
        if hp == 7:
            plan.emit("gpsimd", lambda e: e.dma_start(out=wo_sb[:], in_=d_wo),
                      waits=[w(cP)], inc=cG, amt=16)
    wo_ld = cG.n

    attn_done = cV.n
    attn_v = attn[:].rearrange("p (k n) -> p k n", k=8)
    acc_v = acc[:].rearrange("p (m n) -> p m n", m=8)
    pswo_free = [(cV, 0), (cV, 0)]
    for m in range(8):
        for nt in range(4):
            bi = (m * 4 + nt) % 2
            for kk in range(8):
                plan.emit("tensor", lambda e, m=m, nt=nt, bi=bi, kk=kk: e.matmul(
                    ps_s[bi][:, 0:512],
                    wo_sb[:, 1024 * kk + 128 * m:1024 * kk + 128 * m + 128],
                    attn_v[:, kk, 512 * nt:512 * nt + 512],
                    start=(kk == 0), stop=(kk == 7)),
                    waits=([wv(cG, wo_ld), wv(cV, attn_done), pswo_free[bi],
                            pss_free[bi][0]] if kk == 0 else []),
                    inc=(cP if kk == 7 else None))
            mm_done = cP.n
            plan.emit("vector", lambda e, m=m, nt=nt, bi=bi: e.tensor_copy(
                acc_v[:, m, 512 * nt:512 * nt + 512], ps_s[bi][:, 0:512]),
                waits=[wv(cP, mm_done)], inc=cV)
            pswo_free[bi] = w(cV)
            pss_free[bi][0] = w(cV)
    es_psA.close()
    maybe_tap("t_attn", attn[:])
    maybe_tap("t_wo", acc[:])

    p3_mms_done = cP.n
    plan.emit("gpsimd", lambda e: e.dma_start(out=xn_own[:], in_=d_xn_sav),
              waits=[wv(cP, p3_mms_done)], inc=cG, amt=16)
    xn_reload = cG.n

    barrier()


    es_psF = ExitStack()
    ps_u1 = [pst(f"ps_u1{i}", [128, 1024], F32, es_psF) for i in range(2)]
    ps_u3 = [pst(f"ps_u3{i}", [128, 1024], F32, es_psF) for i in range(2)]
    ps_r = ps_u1
    ps_bcf = ps_u3[0]

    hh_v = h_half[:].rearrange("p (k n) -> p k n", k=8)
    w1_free = [(cP, 0), (cP, 0)]
    w3_free = [(cP, 0), (cP, 0)]
    w2_free = [(cP, 0)]
    u1_free = [(cA, 0), (cA, 0)]
    u3_free = [(cV, 0), (cV, 0)]
    silu_free = [(cV, 0)]
    tmp_free = [(cV, 0)]
    bcast_free = [(cV, 0)]
    w2_ld = [0]
    w13_ld = {}

    def expert_sources(E):
        if E == 0:
            return (lambda mq: d_ws1[mq, :, :], lambda mq: d_ws3[mq, :, :],
                    lambda Hf: d_ws2[Hf, :, :], None)
        ee = E - 1
        return (lambda mq, ee=ee: d_we1[ee, mq, :, :],
                lambda mq, ee=ee: d_we3[ee, mq, :, :],
                lambda Hf, ee=ee: d_we2[ee, Hf, :, :], ee)

    def load_w2(E, Hf):
        _, _, s2, _ = expert_sources(E)
        plan.emit("gpsimd", lambda e, s2=s2, Hf=Hf: e.dma_start(
            out=w2h[:], in_=s2(Hf)), waits=[w2_free[0]], inc=cG, amt=16)
        w2_ld[0] = cG.n

    def load_w13(E, Hf, m):
        i = E * 16 + Hf * 8 + m
        bi = i % 2
        s1, s3, _, _ = expert_sources(E)
        mq = Hf * 8 + m
        plan.emit("gpsimd", lambda e, bi=bi, s1=s1, mq=mq: e.dma_start(
            out=w1c[bi][:], in_=s1(mq)), waits=[w1_free[bi]], inc=cG, amt=16)
        plan.emit("gpsimd", lambda e, bi=bi, s3=s3, mq=mq: e.dma_start(
            out=w3c[bi][:], in_=s3(mq)), waits=[w3_free[bi]], inc=cG, amt=16)
        w13_ld[(E, Hf, m)] = cG.n

    step_list = [(E, Hf) for E in range(9) for Hf in range(2)]
    m_list = [(E, Hf, m) for E, Hf in step_list for m in range(8)]
    load_w13(0, 0, 0)
    load_w13(0, 0, 1)
    bc_done = 0

    for idx, (E, Hf) in enumerate(step_list):
        ee = None if E == 0 else E - 1
        load_w2(E, Hf)
        if ee is not None and Hf == 0:
            for nt in range(4):
                plan.emit("tensor", lambda e, ee=ee, nt=nt: e.matmul(
                    ps_bcf[:, 0:512], sel8[0:8, 128 * ee:128 * ee + 128],
                    wT[0:8, 512 * nt:512 * nt + 512], start=True, stop=True,
                    skip_group_check=True),
                    waits=[u3_free[0]] + ([bcast_free[0]] if nt == 0 else []), inc=cP)
                bcmm = cP.n
                plan.emit("scalar", lambda e, nt=nt: e.copy(
                    out=bcast[:, 512 * nt:512 * nt + 512], in_=ps_bcf[:, 0:512]),
                    waits=[wv(cP, bcmm)], inc=cA)
                u3_free[0] = w(cA)
            bc_done = cA.n
        for m in range(8):
            mi = E * 16 + Hf * 8 + m
            wbi = mi % 2
            wld = w13_ld[(E, Hf, m)]
            for nh in range(2):
                pbi = (m * 2 + nh) % 2
                for kk in range(8):
                    for ns in range(2):
                        plan.emit("tensor", lambda e, kk=kk, ns=ns, wbi=wbi, pbi=pbi,
                                  nh=nh: e.matmul(
                            ps_u1[pbi][:, 512 * ns:512 * ns + 512],
                            w1c[wbi][:, 128 * kk:128 * kk + 128],
                            xo_v[:, kk, 1024 * nh + 512 * ns:1024 * nh + 512 * ns + 512],
                            start=(kk == 0), stop=(kk == 7), skip_group_check=True),
                            waits=([wv(cG, wld), wv(cG, xn_reload), u1_free[pbi]]
                                   if (kk == 0 and ns == 0) else []))
                for kk in range(8):
                    for ns in range(2):
                        last = (kk == 7 and ns == 1)
                        plan.emit("tensor", lambda e, kk=kk, ns=ns, wbi=wbi, pbi=pbi,
                                  nh=nh: e.matmul(
                            ps_u3[pbi][:, 512 * ns:512 * ns + 512],
                            w3c[wbi][:, 128 * kk:128 * kk + 128],
                            xo_v[:, kk, 1024 * nh + 512 * ns:1024 * nh + 512 * ns + 512],
                            start=(kk == 0), stop=(kk == 7), skip_group_check=True),
                            waits=([u3_free[pbi]] if (kk == 0 and ns == 0) else []),
                            inc=(cP if last else None))
                u_done = cP.n
                if nh == 1:
                    w1_free[wbi] = w(cP)
                    w3_free[wbi] = w(cP)
                plan.emit("scalar", lambda e, pbi=pbi: e.activation(
                    out=silu_t[:, 0:1024], in_=ps_u1[pbi][:, 0:1024], func=AF.Silu),
                    waits=[wv(cP, u_done), silu_free[0]], inc=cA)
                u1_free[pbi] = w(cA)
                sl_done = cA.n
                plan.emit("vector", lambda e, m=m, nh=nh, pbi=pbi: e.tensor_tensor(
                    out=hh_v[:, m, 1024 * nh:1024 * nh + 1024], in0=silu_t[:, 0:1024],
                    in1=ps_u3[pbi][:, 0:1024], op=ALU.mult),
                    waits=[wv(cA, sl_done), w(cP)], inc=cV)
                u3_free[pbi] = w(cV)
                silu_free[0] = w(cV)
            nxt = mi + 2
            if nxt < len(m_list):
                En, Hn, mn = m_list[nxt]
                load_w13(En, Hn, mn)
        h_done = cV.n
        if E == 1 and Hf == 0:
            maybe_tap("t_h", h_half[:])
            maybe_tap("t_bc", bcast[:])
        for m in range(8):
            for nt in range(4):
                pbi = (m * 4 + nt) % 2
                for kk in range(8):
                    plan.emit("tensor", lambda e, m=m, nt=nt, kk=kk, pbi=pbi: e.matmul(
                        ps_r[pbi][:, 0:512],
                        w2h[:, 1024 * kk + 128 * m:1024 * kk + 128 * m + 128],
                        hh_v[:, kk, 512 * nt:512 * nt + 512],
                        start=(kk == 0), stop=(kk == 7), skip_group_check=True),
                        waits=([wv(cG, w2_ld[0]), wv(cV, h_done), u1_free[pbi]]
                               if kk == 0 else []),
                        inc=(cP if kk == 7 else None))
                r_done = cP.n
                if E == 0:
                    plan.emit("vector", lambda e, m=m, nt=nt, pbi=pbi: e.scalar_tensor_tensor(
                        out=acc_v[:, m, 512 * nt:512 * nt + 512], in0=ps_r[pbi][:, 0:512],
                        scalar=1.0 / 3.0, in1=acc_v[:, m, 512 * nt:512 * nt + 512],
                        op0=ALU.mult, op1=ALU.add),
                        waits=[wv(cP, r_done)], inc=cV)
                    u1_free[pbi] = w(cV)
                else:
                    plan.emit("vector", lambda e, nt=nt, pbi=pbi: e.tensor_tensor(
                        out=tmp_f[:, 0:512], in0=ps_r[pbi][:, 0:512],
                        in1=bcast[:, 512 * nt:512 * nt + 512], op=ALU.mult),
                        waits=[wv(cP, r_done), wv(cA, bc_done), tmp_free[0]], inc=cV)
                    u1_free[pbi] = w(cV)
                    plan.emit("vector", lambda e, m=m, nt=nt: e.tensor_tensor(
                        out=acc_v[:, m, 512 * nt:512 * nt + 512], in0=tmp_f[:, 0:512],
                        in1=acc_v[:, m, 512 * nt:512 * nt + 512], op=ALU.add),
                        waits=[w(cV)], inc=cV)
                    tmp_free[0] = w(cV)
        if ee is not None:
            bcast_free[0] = w(cV)
        w2_free[0] = w(cP)
        if E == 0 and Hf == 1:
            maybe_tap("t_acc_sh", acc[:])
    es_psF.close()
    maybe_tap("t_acc", acc[:])

    barrier()

    es_p5 = ExitStack()
    ps_o = [pst(f"ps_o{i}", [128, 1024], F32, es_p5) for i in range(4)]
    xo_free2 = [(cV, 0), (cV, 0), (cV, 0), (cV, 0)]
    out_free = [(cS, 0), (cS, 0)]
    pso_free = [(cV, 0)] * 4
    for o in range(OWN):
        bi = o % 2
        b4 = o % 4
        plan.emit("sync", lambda e, o=o, b4=b4: e.dma_start(
            out=xo_buf[b4][:], in_=d_xo[128 * o:128 * o + 128, :]),
            waits=[xo_free2[b4]], inc=cS, amt=16)
        xld = cS.n
        for kk in range(8):
            plan.emit("tensor", lambda e, o=o, kk=kk, b4=b4: e.transpose(
                ps_o[b4][:, 128 * kk:128 * kk + 128],
                acc_v[:, kk, 128 * o:128 * o + 128], ident_f32[:]),
                waits=([pso_free[b4]] if kk == 0 else []), inc=(cP if kk == 7 else None))
        tr_done = cP.n
        plan.emit("vector", lambda e, bi=bi, b4=b4: e.tensor_add(
            out_sb[bi][:], ps_o[b4][:, 0:1024], xo_buf[b4][:]),
            waits=[wv(cP, tr_done), wv(cS, xld), out_free[bi]], inc=cV)
        pso_free[b4] = w(cV)
        xo_free2[b4] = w(cV)
        add_done = cV.n
        plan.emit("sync", lambda e, o=o, bi=bi: e.dma_start(
            out=d_out[128 * o:128 * o + 128, :], in_=out_sb[bi][:]),
            waits=[wv(cV, add_done)], inc=cS, amt=16)
        out_free[bi] = w(cS)
    es_p5.close()

    final_out = cS.n

    def _final_wait(eng, v=final_out):
        eng.wait_ge(cS.sem, v)
    plan.ops["sync"].append(_final_wait)

    with es_root, nc.Block() as block:
        @block.sync
        def _(eng):
            for do in plan.ops["sync"]:
                do(eng)

        @block.tensor
        def _(eng):
            for do in plan.ops["tensor"]:
                do(eng)

        @block.vector
        def _(eng):
            for do in plan.ops["vector"]:
                do(eng)

        @block.scalar
        def _(eng):
            for do in plan.ops["scalar"]:
                do(eng)

        @block.gpsimd
        def _(eng):
            for do in plan.ops["gpsimd"]:
                do(eng)

    return nc



def _pack_lhsT(wt):
    K, M = wt.shape
    return np.ascontiguousarray(
        wt.reshape(K // 128, 128, M).transpose(1, 0, 2).reshape(128, -1))


def _pack_cols(wmat, ncol=128):
    K, Mt = wmat.shape
    out = np.empty((Mt // ncol, 128, (K // 128) * ncol), np.float32)
    for mq in range(Mt // ncol):
        out[mq] = _pack_lhsT(wmat[:, ncol * mq:ncol * mq + ncol])
    return np.ascontiguousarray(out)


def _rope_tables(positions, d=DR):
    inv_freq = (10000.0 ** (-np.arange(0, d, 2, dtype=np.float32) / d))
    ang = positions.astype(np.float32)[:, None] * inv_freq[None, :]
    return np.cos(ang).astype(np.float32), np.sin(ang).astype(np.float32)


def _tile_pack(tab, w):
    n = tab.shape[0] // 128
    return np.ascontiguousarray(
        tab.reshape(n, 128, w).transpose(1, 0, 2).reshape(128, -1))


def _core_masks(e):
    import ml_dtypes
    tri = (np.arange(128)[:, None] <= np.arange(128)[None, :]).astype(np.float32)
    one = np.ones((128, 128), np.float32)
    zero = np.zeros((128, 128), np.float32)

    def half(j, t_off):
        return one if j < t_off else (tri if j == t_off else zero)
    ms = [np.concatenate([half(j, e), half(j, 2 + e)], axis=1) for j in range(4)]
    full = np.concatenate([ms[0], ms[1], ms[2], ms[3]], axis=1)
    return full.astype(ml_dtypes.bfloat16)


_NC_CACHE = {}


def prepare_in_maps(inputs):
    import ml_dtypes
    x = np.ascontiguousarray(np.asarray(inputs["L0_attn_x"], np.float32))
    W = {k: np.ascontiguousarray(np.asarray(v, np.float32)) for k, v in inputs.items()
         if k != "L0_attn_x"}

    shared = {
        "wdqT": _pack_lhsT(W["L0_attn_W_DQ"].T),
        "wdkvT": _pack_lhsT(W["L0_attn_W_DKV"].T),
        "wuqT": _pack_lhsT(W["L0_attn_W_UQ"].T),
        "wqrT": _pack_lhsT(W["L0_attn_W_QR"].T),
        "wukT": _pack_lhsT(W["L0_attn_W_UK"].T),
        "wuvT": _pack_lhsT(W["L0_attn_W_UV"].T),
        "wkrT": _pack_lhsT(W["L0_attn_W_KR"].T),
        "woT": _pack_lhsT(W["L0_attn_W_O"].T),
        "d_wg": _pack_lhsT(W["L0_ffn_Wg"]),
        "ws1": _pack_cols(W["L0_ffn_Ws1"]),
        "ws3": _pack_cols(W["L0_ffn_Ws3"]),
        "ws2": np.stack([_pack_lhsT(W["L0_ffn_Ws2"][1024 * Hf:1024 * Hf + 1024, :])
                         for Hf in range(2)]),
        "we1": np.stack([_pack_cols(W["L0_ffn_We1"][ee]) for ee in range(8)]),
        "we3": np.stack([_pack_cols(W["L0_ffn_We3"][ee]) for ee in range(8)]),
        "we2": np.stack([np.stack(
            [_pack_lhsT(W["L0_ffn_We2"][ee][1024 * Hf:1024 * Hf + 1024, :])
             for Hf in range(2)]) for ee in range(8)]),
        "d_ident_bf": np.eye(128, dtype=ml_dtypes.bfloat16),
        "d_ident_f32": np.eye(128, dtype=np.float32),
        "d_ones64": np.ones((1, 64), dtype=ml_dtypes.bfloat16),
    }
    sel8 = np.zeros((8, 1024), np.float32)
    for ee in range(8):
        sel8[ee, 128 * ee:128 * ee + 128] = 1.0
    shared["d_sel8"] = sel8.astype(ml_dtypes.bfloat16)
    ckc, cks = _rope_tables(np.arange(S))
    shared["ropek_cos"] = _tile_pack(ckc, 16)
    shared["ropek_sin"] = _tile_pack(cks, 16)

    in_maps = []
    metas = []
    for c in range(8):
        b, e = c // 2, c % 2
        own_rows = np.concatenate(
            [np.arange(128 * (2 * s + e), 128 * (2 * s + e) + 128) for s in range(OWN)])
        qc, qs = _rope_tables(own_rows)
        qc16 = np.tile(qc, (1, 16)).astype(np.float32)
        qs16 = np.tile(qs, (1, 16)).astype(np.float32)
        m = dict(shared)
        m["x_seq"] = np.ascontiguousarray(x[b])
        m["x_own"] = np.ascontiguousarray(x[b][own_rows])
        m["d_masks"] = np.asarray(_core_masks(e))
        m["ropeq_cos"] = _tile_pack(qc16, 256)
        m["ropeq_sin"] = _tile_pack(qs16, 256)
        in_maps.append(m)
        metas.append((b, own_rows))
    return in_maps, metas


def kernel(**inputs):
    in_maps, metas = prepare_in_maps(inputs)
    if "prog" not in _NC_CACHE:
        _NC_CACHE["prog"] = build_program()
    nc = _NC_CACHE["prog"]
    res = run_bass_kernel_spmd(nc, in_maps, core_ids=list(range(8)))

    out = np.empty((B, S, D), np.float32)
    for c in range(8):
        b, own_rows = metas[c]
        out[b][own_rows, :] = res.results[c]["out"]
    return out


# version 26
# speedup vs baseline: 1.0938x; 1.0938x over previous
import math
import numpy as np
from contextlib import ExitStack

import concourse.bass as bass
import concourse.mybir as mybir
from concourse.bass_utils import run_bass_kernel_spmd

F32 = mybir.dt.float32
BF16 = mybir.dt.bfloat16
AF = mybir.ActivationFunctionType
ALU = mybir.AluOpType
AX = mybir.AxisListType

B, S, D = 4, 4096, 1024
H, DH, DR, DLAT = 16, 64, 32, 512
NT = S // 128
OWN = 16
TOK = OWN * 128
SCALE = 1.0 / float(np.float32(np.sqrt(np.float32(96.0))))
EHI = float(math.exp(80.0 * SCALE))
ELO = float(math.exp(-80.0 * SCALE))
EPS = 1e-5
BIG = 1e9

ENGINES = ("sync", "tensor", "vector", "scalar", "gpsimd")

_DEBUG_TAPS = []


class Chan:
    def __init__(self, es, nc, name):
        self.sem = es.enter_context(nc.semaphore(name))
        self.n = 0


class Plan:
    def __init__(self):
        self.ops = {e: [] for e in ENGINES}

    def emit(self, engine, fn, waits=(), inc=None, amt=1):
        if inc is not None and amt == 16:
            waits = tuple(waits) + ((inc, inc.n),)
        waits = tuple(waits)

        def do(eng, fn=fn, waits=waits, inc=inc, amt=amt):
            for ch, v in waits:
                if v > 0:
                    eng.wait_ge(ch.sem, v)
            instr = fn(eng)
            if inc is not None:
                instr.then_inc(inc.sem, amt)
        self.ops[engine].append(do)
        if inc is not None:
            inc.n += amt

    def barrier_wait(self, engine, chans):
        snap = [(ch, ch.n) for ch in chans]

        def do(eng, snap=snap):
            for ch, v in snap:
                if v > 0:
                    eng.wait_ge(ch.sem, v)
        self.ops[engine].append(do)


def build_program():
    nc = bass.Bass("TRN2", target_bir_lowering=False, debug=False)

    d_x = nc.dram_tensor("x_seq", [S, D], F32, kind="ExternalInput").ap()
    d_xo = nc.dram_tensor("x_own", [TOK, D], F32, kind="ExternalInput").ap()
    d_masks = nc.dram_tensor("d_masks", [128, 1024], BF16, kind="ExternalInput").ap()
    d_rkc = nc.dram_tensor("ropek_cos", [128, NT * 16], F32, kind="ExternalInput").ap()
    d_rks = nc.dram_tensor("ropek_sin", [128, NT * 16], F32, kind="ExternalInput").ap()
    d_rqc = nc.dram_tensor("ropeq_cos", [128, OWN * 256], F32, kind="ExternalInput").ap()
    d_rqs = nc.dram_tensor("ropeq_sin", [128, OWN * 256], F32, kind="ExternalInput").ap()
    d_idb = nc.dram_tensor("d_ident_bf", [128, 128], BF16, kind="ExternalInput").ap()
    d_idf = nc.dram_tensor("d_ident_f32", [128, 128], F32, kind="ExternalInput").ap()
    d_ones64 = nc.dram_tensor("d_ones64", [1, 64], BF16, kind="ExternalInput").ap()
    d_sel8 = nc.dram_tensor("d_sel8", [8, 1024], BF16, kind="ExternalInput").ap()
    d_wdq = nc.dram_tensor("wdqT", [128, 8 * 512], F32, kind="ExternalInput").ap()
    d_wdkv = nc.dram_tensor("wdkvT", [128, 8 * 512], F32, kind="ExternalInput").ap()
    d_wuq = nc.dram_tensor("wuqT", [128, 4 * 1024], F32, kind="ExternalInput").ap()
    d_wqr = nc.dram_tensor("wqrT", [128, 4 * 512], F32, kind="ExternalInput").ap()
    d_wuk = nc.dram_tensor("wukT", [128, 4 * 1024], F32, kind="ExternalInput").ap()
    d_wuv = nc.dram_tensor("wuvT", [128, 4 * 1024], F32, kind="ExternalInput").ap()
    d_wkr = nc.dram_tensor("wkrT", [128, 8 * 32], F32, kind="ExternalInput").ap()
    d_wo = nc.dram_tensor("woT", [128, 8 * 1024], F32, kind="ExternalInput").ap()
    d_wg = nc.dram_tensor("d_wg", [128, 8 * 8], F32, kind="ExternalInput").ap()
    d_ws1 = nc.dram_tensor("ws1", [16, 128, 1024], F32, kind="ExternalInput").ap()
    d_ws3 = nc.dram_tensor("ws3", [16, 128, 1024], F32, kind="ExternalInput").ap()
    d_ws2 = nc.dram_tensor("ws2", [2, 128, 8 * 1024], F32, kind="ExternalInput").ap()
    d_we1 = nc.dram_tensor("we1", [8, 16, 128, 1024], F32, kind="ExternalInput").ap()
    d_we3 = nc.dram_tensor("we3", [8, 16, 128, 1024], F32, kind="ExternalInput").ap()
    d_we2 = nc.dram_tensor("we2", [8, 2, 128, 8 * 1024], F32, kind="ExternalInput").ap()
    d_out = nc.dram_tensor("out", [TOK, D], F32, kind="ExternalOutput").ap()
    d_xn_sav = nc.dram_tensor("xn_sav", [128, 8 * TOK], BF16, kind="Internal").ap()

    _TAP_DEFS = {
        "t_xn_own": ([128, 8 * TOK], BF16),
        "t_gates": ([128, 128], F32),
        "t_wt": ([8, TOK], BF16),
        "t_ckv": ([128, 4 * S], BF16),
        "t_krf": ([32, S], BF16),
        "t_q": ([96, 16 * TOK], BF16),
        "t_attn": ([128, 8 * TOK], BF16),
        "t_wo": ([128, 8 * TOK], F32),
        "t_acc": ([128, 8 * TOK], F32),
        "t_acc_sh": ([128, 8 * TOK], F32),
        "t_h": ([128, 8 * TOK], BF16),
        "t_bc": ([128, TOK], BF16),
    }
    taps = {}
    for name in _DEBUG_TAPS:
        shape, dt = _TAP_DEFS[name]
        taps[name] = nc.dram_tensor(name, shape, dt, kind="ExternalOutput").ap()

    es_root = ExitStack()
    plan = Plan()

    def sb(name, shape, dt, off):
        return nc.alloc_sbuf_tensor_at(name, shape, dt, offset=off)

    def pst(name, shape, dt, stack):
        return stack.enter_context(nc.psum_tensor(name, shape, dt))

    cS = Chan(es_root, nc, "cS")
    cG = Chan(es_root, nc, "cG")
    cP = Chan(es_root, nc, "cP")
    cV = Chan(es_root, nc, "cV")
    cA = Chan(es_root, nc, "cA")
    ALL_CH = (cS, cG, cP, cV, cA)

    def w(ch):
        return (ch, ch.n)

    def wv(ch, val):
        return (ch, val)

    def barrier():
        for e in ENGINES:
            plan.barrier_wait(e, ALL_CH)

    def maybe_tap(name, src_ap):
        if name not in taps:
            return
        plan.barrier_wait("sync", ALL_CH)
        plan.emit("sync", lambda e, src_ap=src_ap, name=name: e.dma_start(
            out=taps[name], in_=src_ap), inc=cS, amt=16)
        barrier()

    BASE = 16512
    xn_own = sb("xn_own", [128, 8 * TOK], BF16, BASE + 0)
    attn = sb("attn", [128, 8 * TOK], BF16, BASE + 0)
    ident_bf = sb("ident_bf", [128, 128], BF16, BASE + 32768)
    ident_f32 = sb("ident_f32", [128, 128], F32, BASE + 33024)
    ones64 = sb("ones64", [1, 64], BF16, BASE + 33536)
    wg = sb("wg", [128, 64], F32, BASE + 33664)
    sel8 = sb("sel8", [8, 1024], BF16, BASE + 33920)
    masks = sb("masks", [128, 1024], BF16, BASE + 35968)
    wT = sb("wT", [8, TOK], BF16, BASE + 38016)
    eps_col = sb("eps_col", [128, 1], F32, BASE + 42112)
    stats = sb("stats", [128, 32], F32, BASE + 42176)
    gates = sb("gates", [128, 128], F32, BASE + 42304)
    w_tok = sb("w_tok", [128, 128], F32, BASE + 42816)
    gates_s = w_tok
    g_tmp = sb("g_tmp", [128, 8], F32, BASE + 43328)
    g_tmp2 = sb("g_tmp2", [128, 8], F32, BASE + 43392)
    wuk = sb("wuk", [128, 4096], BF16, BASE + 43904)
    wuv = sb("wuv", [128, 4096], BF16, BASE + 52096)
    wkr = sb("wkr", [128, 256], BF16, BASE + 60288)
    w2h = sb("w2h", [128, 8192], BF16, BASE + 43904)
    wdq = sb("wdq", [128, 4096], BF16, BASE + 60800)
    wdkv = sb("wdkv", [128, 4096], BF16, BASE + 68992)
    wuq = sb("wuq", [128, 4096], BF16, BASE + 77184)
    wqr = sb("wqr", [128, 2048], BF16, BASE + 85376)
    k_buf0 = sb("k_buf0", [96, S], BF16, BASE + 60800)
    k_buf1 = sb("k_buf1", [96, S], BF16, BASE + 68992)
    wo_sb = sb("wo_sb", [128, 8192], BF16, BASE + 60800)
    w1c = [sb(f"w1c{i}", [128, 1024], BF16, BASE + 60800 + 2048 * i) for i in range(2)]
    w3c = [sb(f"w3c{i}", [128, 1024], BF16, BASE + 64896 + 2048 * i) for i in range(2)]
    silu_t = sb("silu_t", [128, 1024], BF16, BASE + 68992)
    bcast = sb("bcast", [128, TOK], BF16, BASE + 71040)
    tmp_f = sb("tmp_f", [128, 512], F32, BASE + 75136)
    xn_fm = sb("xn_fm", [128, 8 * S], BF16, BASE + 89472)
    q_all = sb("q_all", [96, 16 * TOK], BF16, BASE + 89472)
    acc = sb("acc", [128, 8 * TOK], F32, BASE + 89472)
    c_kv = sb("c_kv", [128, 4 * S], BF16, BASE + 155008)
    h_half = sb("h_half", [128, 8 * TOK], BF16, BASE + 155008)
    x_tile = [sb(f"x_tile{i}", [128, D], F32, BASE + 187776 + 4096 * i) for i in range(4)]
    xnf_buf2 = [sb(f"xnf_buf{i}", [128, D], F32, BASE + 155008 + 4096 * i) for i in range(2)]
    xnb_buf = [sb(f"xnb_buf{i}", [128, D], BF16, BASE + 163200 + 2048 * i) for i in range(4)]
    stripe = [sb(f"stripe{i}", [128, 1024], F32, BASE + 171392 + 4096 * i) for i in range(2)]
    stripe2 = stripe
    sq_scr = None
    FB = BASE + 187776
    k_rf = sb("k_rf", [32, S], BF16, FB)
    rkc = sb("rkc", [128, NT * 16], F32, FB + 20608)
    rks = sb("rks", [128, NT * 16], F32, FB + 22656)
    krt_buf2 = [sb(f"krt_buf{i}", [128, 32], BF16, FB + 12288 + 64 * i) for i in range(2)]
    r_t1 = sb("r_t1", [128, 256], F32, FB + 12544)
    r_t2 = sb("r_t2", [128, 256], F32, FB + 13568)
    r_t3 = sb("r_t3", [128, 256], F32, FB + 14592)
    r_t4 = sb("r_t4", [128, 256], F32, FB + 15616)
    c_qh = sb("c_qh", [128, 4 * 1024], BF16, FB + 8192)
    rq_c = [sb(f"rq_c{i}", [128, 256], F32, FB + 16384 + 1024 * i) for i in range(2)]
    rq_s = [sb(f"rq_s{i}", [128, 256], F32, FB + 18432 + 1024 * i) for i in range(2)]
    qr_buf2 = [sb(f"qr_buf{i}", [128, 512], BF16, FB + 20480 + 1024 * i) for i in range(2)]
    q_t1 = sb("q_t1", [128, 256], F32, FB + 22528)
    q_t2 = sb("q_t2", [128, 256], F32, FB + 23552)
    v_aug = [sb(f"v_aug{i}", [128, 65 * NT], BF16, FB + 8192 + 4224 * i) for i in range(2)]
    pt_buf = [sb(f"pt_buf{i}", [128, 1024], BF16, FB + 16640 + 2048 * i) for i in range(3)]
    rsum = sb("rsum", [1, 256], F32, FB + 22784)
    rsum_bf = sb("rsum_bf", [1, 256], BF16, FB + 23808)
    bc_sb = sb("bc_sb", [64, 256], BF16, FB + 24320)
    k_buf = [k_buf0, k_buf1]
    xo_buf = [sb(f"xo_buf{i}", [128, D], F32, FB + 4096 * i) for i in range(4)]
    out_sb = [sb(f"out_sb{i}", [128, D], F32, FB + 16384 + 4096 * i) for i in range(2)]

    plan.emit("sync", lambda e: e.dma_start(out=ident_bf[:], in_=d_idb), inc=cS, amt=16)
    plan.emit("sync", lambda e: e.dma_start(out=ident_f32[:], in_=d_idf), inc=cS, amt=16)
    plan.emit("sync", lambda e: e.dma_start(out=ones64[:], in_=d_ones64), inc=cS, amt=16)
    plan.emit("sync", lambda e: e.dma_start(out=sel8[:], in_=d_sel8), inc=cS, amt=16)
    plan.emit("sync", lambda e: e.dma_start(out=masks[:], in_=d_masks), inc=cS, amt=16)
    plan.emit("sync", lambda e: e.dma_start(out=rkc[:], in_=d_rkc), inc=cS, amt=16)
    plan.emit("sync", lambda e: e.dma_start(out=rks[:], in_=d_rks), inc=cS, amt=16)
    consts_done = cS.n
    plan.emit("vector", lambda e: e.memset(eps_col[:], EPS), inc=cV)

    for dst, src in ((wdq, d_wdq), (wdkv, d_wdkv), (wuq, d_wuq), (wqr, d_wqr),
                     (wuk, d_wuk), (wuv, d_wuv), (wkr, d_wkr)):
        plan.emit("gpsimd", lambda e, dst=dst, src=src: e.dma_start(out=dst[:], in_=src),
                  inc=cG, amt=16)
    plan.emit("gpsimd", lambda e: e.dma_start(out=wg[:], in_=d_wg), inc=cG, amt=16)
    attn_w_done = cG.n

    es_ps0 = ExitStack()
    ps_tr = [pst(f"ps_tr{i}", [128, 1024], BF16, es_ps0) for i in range(4)]
    ps_trf = [pst("ps_trf0", [128, 1024], F32, es_ps0)]
    ps_gate = pst("ps_gate", [128, 512], F32, es_ps0)
    ps_wt = pst("ps_wt", [128, 512], F32, es_ps0)

    x_free = [[], [], [], []]
    xnb_free = [(cP, 0)] * 4
    xnf_free = [(cP, 0), (cP, 0)]
    stripe_free = [(cP, 0), (cP, 0)]
    pstr_free = [(cA, 0)] * 4
    pstrf_free = [(cA, 0)]
    gate_ps_free = [(cA, 0)]
    xn_tile_done = []

    def p0_stageA(i, src_rows):
        b4 = i % 4
        bi = i % 2
        sc = 8 * b4
        plan.emit("sync", lambda e: e.dma_start(out=x_tile[b4][:], in_=src_rows),
                  waits=list(x_free[b4]), inc=cS, amt=16)
        ld = cS.n
        plan.emit("vector", lambda e: e.reduce_sum(stats[:, sc:sc + 1], x_tile[b4][:],
                                                   axis=AX.X),
                  waits=[wv(cS, ld)], inc=cV)
        plan.emit("scalar", lambda e: e.activation(
            out=stripe[bi][:], in_=x_tile[b4][:], func=AF.Square,
            accum_out=stats[:, sc + 2:sc + 3]),
            waits=[wv(cS, ld), stripe_free[bi]], inc=cA)
        sq[i] = cA.n
        plan.emit("vector", lambda e: e.tensor_scalar(
            out=stats[:, sc + 1:sc + 2], in0=stats[:, sc:sc + 1], scalar1=1.0 / D,
            scalar2=None, op0=ALU.mult), waits=[w(cV)], inc=cV)
        plan.emit("vector", lambda e: e.tensor_mul(
            stats[:, sc + 5:sc + 6], stats[:, sc + 1:sc + 2], stats[:, sc + 1:sc + 2]),
            waits=[w(cV)], inc=cV)

    def p0_stageB(i):
        b4 = i % 4
        sc = 8 * b4
        plan.emit("vector", lambda e: e.tensor_scalar(
            out=stats[:, sc + 3:sc + 4], in0=stats[:, sc + 2:sc + 3], scalar1=1.0 / D,
            scalar2=None, op0=ALU.mult), waits=[w(cV), wv(cA, sq[i])], inc=cV)
        plan.emit("vector", lambda e: e.tensor_sub(
            stats[:, sc + 3:sc + 4], stats[:, sc + 3:sc + 4], stats[:, sc + 5:sc + 6]),
            waits=[w(cV)], inc=cV)
        var_done = cV.n
        plan.emit("scalar", lambda e: e.activation(
            out=stats[:, sc + 4:sc + 5], in_=stats[:, sc + 3:sc + 4], func=AF.Sqrt,
            bias=eps_col[:, :]), waits=[wv(cV, var_done)], inc=cA)
        sqr[i] = cA.n

    def p0_stageC(i, dst_fm, dst_cols_base, own_idx):
        b4 = i % 4
        bi = i % 2
        sc = 8 * b4
        plan.emit("vector", lambda e: e.reciprocal(stats[:, sc + 6:sc + 7],
                                                   stats[:, sc + 4:sc + 5]),
                  waits=[wv(cA, sqr[i])], inc=cV)
        plan.emit("vector", lambda e: e.scalar_tensor_tensor(
            out=stats[:, sc + 7:sc + 8], in0=stats[:, sc + 1:sc + 2], scalar=-1.0,
            in1=stats[:, sc + 6:sc + 7], op0=ALU.mult, op1=ALU.mult),
            waits=[w(cV)], inc=cV)
        plan.emit("vector", lambda e: e.tensor_scalar(
            out=xnb_buf[b4][:], in0=x_tile[b4][:], scalar1=stats[:, sc + 6:sc + 7],
            scalar2=stats[:, sc + 7:sc + 8], op0=ALU.mult, op1=ALU.add),
            waits=[w(cV), xnb_free[b4]], inc=cV)
        xnb_done = cV.n
        xnf_done = 0
        if own_idx is not None:
            plan.emit("vector", lambda e: e.tensor_scalar(
                out=xnf_buf2[bi][:], in0=x_tile[b4][:], scalar1=stats[:, sc + 6:sc + 7],
                scalar2=stats[:, sc + 7:sc + 8], op0=ALU.mult, op1=ALU.add),
                waits=[w(cV), xnf_free[bi]], inc=cV)
            xnf_done = cV.n
        x_free[b4] = [wv(cV, cV.n), wv(cA, sq[i])]
        for kk in range(8):
            plan.emit("tensor", lambda e, kk=kk: e.transpose(
                ps_tr[b4][:, 128 * kk:128 * kk + 128],
                xnb_buf[b4][:, 128 * kk:128 * kk + 128], ident_bf[:]),
                waits=([wv(cV, xnb_done), pstr_free[b4], wv(cS, consts_done)]
                       if kk == 0 else []),
                inc=(cP if kk == 7 else None))
        xnb_free[b4] = w(cP)
        tr_done = cP.n
        dst_view = dst_fm[:].rearrange("p (k n) -> p k n", k=8)
        plan.emit("scalar", lambda e, dst_view=dst_view: e.copy(
            out=dst_view[:, :, dst_cols_base:dst_cols_base + 128],
            in_=ps_tr[b4][:].rearrange("p (k n) -> p k n", k=8)),
            waits=[wv(cP, tr_done)], inc=cA)
        pstr_free[b4] = w(cA)
        xn_tile_done.append(cA.n)
        if own_idx is None:
            return
        o = own_idx
        for kk in range(8):
            plan.emit("tensor", lambda e, kk=kk: e.transpose(
                ps_trf[0][:, 128 * kk:128 * kk + 128],
                xnf_buf2[bi][:, 128 * kk:128 * kk + 128], ident_f32[:]),
                waits=([wv(cV, xnf_done), pstrf_free[0]] if kk == 0 else []),
                inc=(cP if kk == 7 else None))
        xnf_free[bi] = w(cP)
        trf_done = cP.n
        sq_guard = sq.get(o + NT + 2, sq[o + NT])
        plan.emit("scalar", lambda e: e.copy(out=stripe2[bi][:], in_=ps_trf[0][:]),
                  waits=[wv(cP, trf_done), wv(cA, sq_guard)], inc=cA)
        pstrf_free[0] = w(cA)
        st_done = cA.n
        for kk in range(8):
            plan.emit("tensor", lambda e, kk=kk: e.matmul(
                ps_gate[0:128, 0:8], stripe2[bi][:, 128 * kk:128 * kk + 128],
                wg[:, 8 * kk:8 * kk + 8], start=(kk == 0), stop=(kk == 7)),
                waits=([wv(cA, st_done), gate_ps_free[0], wv(cG, attn_w_done)]
                       if kk == 0 else []),
                inc=(cP if kk == 7 else None))
        stripe_free[bi] = w(cP)
        gmm_done = cP.n
        plan.emit("scalar", lambda e, o=o: e.copy(
            out=gates[:, 8 * o:8 * o + 8], in_=ps_gate[0:128, 0:8]),
            waits=[wv(cP, gmm_done)], inc=cA)
        gate_ps_free[0] = w(cA)

    sq = {}
    sqr = {}
    tiles = ([(t, d_x[128 * t:128 * t + 128, :], xn_fm, 128 * t, None) for t in range(NT)] +
             [(NT + o, d_xo[128 * o:128 * o + 128, :], xn_own, 128 * o, o)
              for o in range(OWN)])
    for i in range(len(tiles) + 4):
        if 4 <= i and (i - 4) < len(tiles):
            tj = tiles[i - 4]
            p0_stageC(tj[0], tj[2], tj[3], tj[4])
        if 2 <= i and (i - 2) < len(tiles):
            p0_stageB(tiles[i - 2][0])
        if i < len(tiles):
            ti = tiles[i]
            p0_stageA(ti[0], ti[1])
    p0_done_A = cA.n

    maybe_tap("t_xn_own", xn_own[:])
    plan.emit("sync", lambda e: e.dma_start(out=d_xn_sav, in_=xn_own[:]),
              waits=[wv(cA, p0_done_A)], inc=cS, amt=16)
    xn_saved = cS.n

    plan.emit("scalar", lambda e: e.activation(out=gates_s[:], in_=gates[:], func=AF.Sigmoid),
              waits=[wv(cA, p0_done_A)], inc=cA)
    sig_done = cA.n
    wt_ps_free = [(cA, 0)]
    for o in range(OWN):
        gsl = gates_s[:, 8 * o:8 * o + 8]
        plan.emit("vector", lambda e, gsl=gsl: e.reduce_max(stats[:, 0:1], gsl, axis=AX.X),
                  waits=[wv(cA, sig_done), w(cV)], inc=cV)
        plan.emit("vector", lambda e, gsl=gsl: e.tensor_scalar(
            out=g_tmp[:], in0=gsl, scalar1=stats[:, 0:1], scalar2=None, op0=ALU.is_ge),
            waits=[w(cV)], inc=cV)
        plan.emit("vector", lambda e: e.tensor_scalar(
            out=g_tmp[:], in0=g_tmp[:], scalar1=BIG, scalar2=None, op0=ALU.mult),
            waits=[w(cV)], inc=cV)
        plan.emit("vector", lambda e, gsl=gsl: e.tensor_sub(g_tmp2[:], gsl, g_tmp[:]),
                  waits=[w(cV)], inc=cV)
        plan.emit("vector", lambda e: e.reduce_max(stats[:, 1:2], g_tmp2[:], axis=AX.X),
                  waits=[w(cV)], inc=cV)
        plan.emit("vector", lambda e, gsl=gsl: e.tensor_scalar(
            out=g_tmp[:], in0=gsl, scalar1=stats[:, 1:2], scalar2=None, op0=ALU.is_ge),
            waits=[w(cV)], inc=cV)
        plan.emit("vector", lambda e, gsl=gsl: e.tensor_mul(g_tmp[:], g_tmp[:], gsl),
                  waits=[w(cV)], inc=cV)
        plan.emit("vector", lambda e: e.tensor_scalar(
            out=stats[:, 2:3], in0=stats[:, 0:1], scalar1=stats[:, 1:2], scalar2=3.0,
            op0=ALU.add, op1=ALU.mult), waits=[w(cV)], inc=cV)
        plan.emit("vector", lambda e: e.reciprocal(stats[:, 3:4], stats[:, 2:3]),
                  waits=[w(cV)], inc=cV)
        plan.emit("vector", lambda e, o=o: e.tensor_scalar(
            out=w_tok[:, 8 * o:8 * o + 8], in0=g_tmp[:], scalar1=stats[:, 3:4], scalar2=None,
            op0=ALU.mult), waits=[w(cV)], inc=cV)
        wt_done = cV.n
        plan.emit("tensor", lambda e, o=o: e.transpose(
            ps_wt[0:8, 0:128], w_tok[:, 8 * o:8 * o + 8], ident_f32[:]),
            waits=[wv(cV, wt_done), wt_ps_free[0]], inc=cP)
        tw_done = cP.n
        plan.emit("scalar", lambda e, o=o: e.copy(
            out=wT[0:8, 128 * o:128 * o + 128], in_=ps_wt[0:8, 0:128]),
            waits=[wv(cP, tw_done)], inc=cA)
        wt_ps_free[0] = w(cA)
    es_ps0.close()
    maybe_tap("t_gates", gates_s[:])
    maybe_tap("t_wt", wT[:])

    es_ps1 = ExitStack()
    ps_m = [pst(f"ps_m{i}", [128, 512], F32, es_ps1) for i in range(4)]
    psm_free = [(cA, 0)] * 4
    xn_fm_v = xn_fm[:].rearrange("p (k n) -> p k n", k=8)
    ckv_v = c_kv[:].rearrange("p (m n) -> p m n", m=4)

    for nt in range(8):
        for m in range(4):
            bi = m
            for kk in range(8):
                plan.emit("tensor", lambda e, nt=nt, m=m, bi=bi, kk=kk: e.matmul(
                    ps_m[bi][:, 0:512],
                    wdkv[:, 512 * kk + 128 * m:512 * kk + 128 * m + 128],
                    xn_fm_v[:, kk, 512 * nt:512 * nt + 512],
                    start=(kk == 0), stop=(kk == 7)),
                    waits=([wv(cG, attn_w_done), wv(cA, p0_done_A), psm_free[bi]]
                           if kk == 0 else []),
                    inc=(cP if kk == 7 else None))
            mm_done = cP.n
            plan.emit("scalar", lambda e, nt=nt, m=m, bi=bi: e.copy(
                out=ckv_v[:, m, 512 * nt:512 * nt + 512], in_=ps_m[bi][:, 0:512]),
                waits=[wv(cP, mm_done)], inc=cA)
            psm_free[bi] = w(cA)

    es_kr = ExitStack()
    ps_kr = [pst(f"ps_kr{i}", [128, 512], F32, es_kr) for i in range(2)]
    ps_krt = pst("ps_krt", [128, 1024], BF16, es_kr)
    pskr_free = [(cV, 0), (cV, 0)]
    krt_free = [(cP, 0), (cP, 0)]
    pskrt_free = [(cA, 0)]
    for t in range(NT):
        kbi = t % 2
        for kk in range(8):
            plan.emit("tensor", lambda e, t=t, kk=kk, kbi=kbi: e.matmul(
                ps_kr[kbi][:, 0:32], xn_fm_v[:, kk, 128 * t:128 * t + 128],
                wkr[:, 32 * kk:32 * kk + 32], start=(kk == 0), stop=(kk == 7)),
                waits=([pskr_free[kbi]] if kk == 0 else []), inc=(cP if kk == 7 else None))
        kr_done = cP.n
        cosv = rkc[:, 16 * t:16 * t + 16]
        sinv = rks[:, 16 * t:16 * t + 16]
        x1 = ps_kr[kbi][:, 0:32].rearrange("p (n two) -> p n two", two=2)[:, :, 0]
        x2 = ps_kr[kbi][:, 0:32].rearrange("p (n two) -> p n two", two=2)[:, :, 1]
        oe = krt_buf2[kbi][:, 0:32].rearrange("p (n two) -> p n two", two=2)[:, :, 0]
        oo = krt_buf2[kbi][:, 0:32].rearrange("p (n two) -> p n two", two=2)[:, :, 1]
        plan.emit("vector", lambda e, x1=x1, cosv=cosv: e.tensor_mul(r_t1[:, 0:16], x1, cosv),
                  waits=[wv(cP, kr_done), wv(cS, consts_done), krt_free[kbi]], inc=cV)
        plan.emit("vector", lambda e, x2=x2, sinv=sinv: e.tensor_mul(r_t2[:, 0:16], x2, sinv),
                  inc=cV)
        plan.emit("vector", lambda e, x1=x1, sinv=sinv: e.tensor_mul(r_t3[:, 0:16], x1, sinv),
                  inc=cV)
        plan.emit("vector", lambda e, x2=x2, cosv=cosv: e.tensor_mul(r_t4[:, 0:16], x2, cosv),
                  inc=cV)
        plan.emit("vector", lambda e, oe=oe: e.tensor_sub(oe, r_t1[:, 0:16], r_t2[:, 0:16]),
                  waits=[w(cV)], inc=cV)
        plan.emit("vector", lambda e, oo=oo: e.tensor_add(oo, r_t3[:, 0:16], r_t4[:, 0:16]),
                  inc=cV)
        rope_done = cV.n
        pskr_free[kbi] = w(cV)
        plan.emit("tensor", lambda e, t=t, kbi=kbi: e.transpose(
            ps_krt[0:32, 0:128], krt_buf2[kbi][:, 0:32], ident_bf[:]),
            waits=[wv(cV, rope_done), pskrt_free[0]], inc=cP)
        krt_free[kbi] = w(cP)
        tkr_done = cP.n
        plan.emit("scalar", lambda e, t=t: e.copy(
            out=k_rf[0:32, 128 * t:128 * t + 128], in_=ps_krt[0:32, 0:128]),
            waits=[wv(cP, tkr_done)], inc=cA)
        pskrt_free[0] = w(cA)
    kr_v_all = cV.n
    es_kr.close()
    maybe_tap("t_ckv", c_kv[:])
    maybe_tap("t_krf", k_rf[:])

    xo_v = xn_own[:].rearrange("p (k n) -> p k n", k=8)
    cq_v = c_qh[:].rearrange("p (m n) -> p m n", m=4)
    es_qr = ExitStack()
    ps_qr = [pst(f"ps_qr{i}", [128, 512], F32, es_qr) for i in range(2)]
    ps_qrt = [pst(f"ps_qrt{i}", [128, 1024], BF16, es_qr) for i in range(2)]
    psqr_free = [(cV, 0), (cV, 0)]
    qrb_free = [(cP, 0), (cP, 0)]
    psqrt_free = [(cV, 0), (cV, 0)]
    rq_free = [(cV, 0), (cV, 0)]
    cq_free = [(cP, 0)]
    cq_done = {}
    for half in range(2):
        tb = 1024 * half
        for nt in range(2):
            for m in range(4):
                bi = m
                for kk in range(8):
                    plan.emit("tensor", lambda e, nt=nt, m=m, bi=bi, kk=kk, tb=tb: e.matmul(
                        ps_m[bi][:, 0:512],
                        wdq[:, 512 * kk + 128 * m:512 * kk + 128 * m + 128],
                        xo_v[:, kk, tb + 512 * nt:tb + 512 * nt + 512],
                        start=(kk == 0), stop=(kk == 7)),
                        waits=([psm_free[bi]] + ([cq_free[0]] if (nt == 0 and kk == 0) else [])
                               if kk == 0 else []),
                        inc=(cP if kk == 7 else None))
                mm_done = cP.n
                plan.emit("scalar", lambda e, nt=nt, m=m, bi=bi: e.copy(
                    out=cq_v[:, m, 512 * nt:512 * nt + 512], in_=ps_m[bi][:, 0:512]),
                    waits=[wv(cP, mm_done)], inc=cA)
                psm_free[bi] = w(cA)
            cq_done[nt] = cA.n
        for hp in range(8):
            for nt in range(2):
                bi = nt
                for kk in range(4):
                    plan.emit("tensor", lambda e, hp=hp, nt=nt, bi=bi, kk=kk: e.matmul(
                        ps_m[bi][:, 0:512],
                        wuq[:, 1024 * kk + 128 * hp:1024 * kk + 128 * hp + 128],
                        cq_v[:, kk, 512 * nt:512 * nt + 512],
                        start=(kk == 0), stop=(kk == 3)),
                        waits=([psm_free[bi], wv(cA, cq_done[nt])] if kk == 0 else []),
                        inc=(cP if kk == 3 else None))
                mm_done = cP.n
                plan.emit("scalar", lambda e, hp=hp, nt=nt, bi=bi, tb=tb: e.copy(
                    out=q_all[0:64, TOK * (2 * hp) + tb + 512 * nt:
                              TOK * (2 * hp) + tb + 512 * nt + 512],
                    in_=ps_m[bi][0:64, 0:512]), waits=[wv(cP, mm_done)], inc=cA)
                plan.emit("scalar", lambda e, hp=hp, nt=nt, bi=bi, tb=tb: e.copy(
                    out=q_all[0:64, TOK * (2 * hp + 1) + tb + 512 * nt:
                              TOK * (2 * hp + 1) + tb + 512 * nt + 512],
                    in_=ps_m[bi][64:128, 0:512]), inc=cA)
                psm_free[bi] = w(cA)
        for oo in range(8):
            o = 8 * half + oo
            bi = o % 2
            plan.emit("sync", lambda e, o=o, bi=bi: e.dma_start(
                out=rq_c[bi][:], in_=d_rqc[:, 256 * o:256 * o + 256]),
                waits=[rq_free[bi]] + ([wv(cV, kr_v_all)] if o < 2 else []),
                inc=cS, amt=16)
            plan.emit("sync", lambda e, o=o, bi=bi: e.dma_start(
                out=rq_s[bi][:], in_=d_rqs[:, 256 * o:256 * o + 256]), inc=cS, amt=16)
            rq_ld = cS.n
            qbi = o % 2
            for kk in range(4):
                plan.emit("tensor", lambda e, oo=oo, kk=kk, qbi=qbi: e.matmul(
                    ps_qr[qbi][:, 0:512], cq_v[:, kk, 128 * oo:128 * oo + 128],
                    wqr[:, 512 * kk:512 * kk + 512], start=(kk == 0), stop=(kk == 3)),
                    waits=([psqr_free[qbi], wv(cA, cq_done[oo // 4])] if kk == 0 else []),
                    inc=(cP if kk == 3 else None))
            qr_done = cP.n
            x1 = ps_qr[qbi][:, 0:512].rearrange("p (n two) -> p n two", two=2)[:, :, 0]
            x2 = ps_qr[qbi][:, 0:512].rearrange("p (n two) -> p n two", two=2)[:, :, 1]
            oe = qr_buf2[qbi][:, 0:512].rearrange("p (n two) -> p n two", two=2)[:, :, 0]
            oo_v = qr_buf2[qbi][:, 0:512].rearrange("p (n two) -> p n two", two=2)[:, :, 1]
            plan.emit("vector", lambda e, x1=x1, bi=bi: e.tensor_mul(q_t1[:], x1, rq_c[bi][:]),
                      waits=[wv(cP, qr_done), wv(cS, rq_ld), qrb_free[qbi]], inc=cV)
            plan.emit("vector", lambda e, x2=x2, bi=bi: e.tensor_mul(q_t2[:], x2, rq_s[bi][:]),
                      inc=cV)
            plan.emit("vector", lambda e, oe=oe: e.tensor_sub(oe, q_t1[:], q_t2[:]),
                      waits=[w(cV)], inc=cV)
            plan.emit("vector", lambda e, x1=x1, bi=bi: e.tensor_mul(q_t1[:], x1, rq_s[bi][:]),
                      waits=[w(cV)], inc=cV)
            plan.emit("vector", lambda e, x2=x2, bi=bi: e.tensor_mul(q_t2[:], x2, rq_c[bi][:]),
                      inc=cV)
            plan.emit("vector", lambda e, oo_v=oo_v: e.tensor_add(oo_v, q_t1[:], q_t2[:]),
                      waits=[w(cV)], inc=cV)
            rope_done = cV.n
            rq_free[bi] = w(cV)
            psqr_free[qbi] = w(cV)
            for hh in range(8):
                tb = hh % 2
                plan.emit("tensor", lambda e, hh=hh, qbi=qbi, tb=tb: e.transpose(
                    ps_qrt[tb][0:64, 0:128], qr_buf2[qbi][:, 64 * hh:64 * hh + 64],
                    ident_bf[:]),
                    waits=([wv(cV, rope_done), psqrt_free[tb]] if hh < 2
                           else [psqrt_free[tb]]), inc=cP)
                t_done = cP.n
                plan.emit("vector", lambda e, o=o, hh=hh, tb=tb: e.tensor_copy(
                    q_all[64:96, TOK * (2 * hh) + 128 * o:TOK * (2 * hh) + 128 * o + 128],
                    ps_qrt[tb][0:32, 0:128]), waits=[wv(cP, t_done)], inc=cV)
                plan.emit("vector", lambda e, o=o, hh=hh, tb=tb: e.tensor_copy(
                    q_all[64:96, TOK * (2 * hh + 1) + 128 * o:
                          TOK * (2 * hh + 1) + 128 * o + 128],
                    ps_qrt[tb][32:64, 0:128]), inc=cV)
                psqrt_free[tb] = w(cV)
            qrb_free[qbi] = w(cP)
        cq_free[0] = w(cP)
    es_qr.close()
    es_ps1.close()
    maybe_tap("t_q", q_all[:])

    barrier()

    es_psA = ExitStack()
    ps_s = [pst(f"ps_s{i}", [128, 1024], F32, es_psA) for i in range(2)]
    ps_at2 = [pst(f"ps_at{i}", [128, 512], F32, es_psA) for i in range(2)]
    ps_bc = pst("ps_bc", [128, 512], F32, es_psA)
    ps_kv = [ps_s[0], ps_s[1]]

    kbuf_free = [(cP, 0), (cP, 0)]
    vaug_free = [(cP, 0), (cP, 0)]
    pss_free = [[(cA, 0)], [(cA, 0)]]
    pskv_free = [pss_free[0], pss_free[1]]
    pt_free = [(cP, 0), (cP, 0), (cP, 0)]
    psat_free = [[(cV, 0)], [(cV, 0)]]
    psbc_free = [(cV, 0)]

    def build_pair(hp):
        for nt in range(8):
            bi = nt % 2
            for kk in range(4):
                plan.emit("tensor", lambda e, hp=hp, nt=nt, bi=bi, kk=kk: e.matmul(
                    ps_kv[bi][:, 0:512],
                    wuk[:, 1024 * kk + 128 * hp:1024 * kk + 128 * hp + 128],
                    ckv_v[:, kk, 512 * nt:512 * nt + 512],
                    start=(kk == 0), stop=(kk == 3)),
                    waits=(([pskv_free[bi][0]] +
                            ([kbuf_free[0], kbuf_free[1], wv(cS, xn_saved)]
                             if nt == 0 else [])) if kk == 0 else []),
                    inc=(cP if kk == 3 else None))
            mm_done = cP.n
            plan.emit("vector", lambda e, nt=nt, bi=bi: e.tensor_copy(
                k_buf[0][0:64, 512 * nt:512 * nt + 512], ps_kv[bi][0:64, 0:512]),
                waits=[wv(cP, mm_done)], inc=cV)
            plan.emit("vector", lambda e, nt=nt, bi=bi: e.tensor_copy(
                k_buf[1][0:64, 512 * nt:512 * nt + 512], ps_kv[bi][64:128, 0:512]),
                inc=cV)
            pskv_free[bi][0] = w(cV)
        plan.emit("vector", lambda e: e.tensor_copy(k_buf[0][64:96, :], k_rf[0:32, :]),
                  waits=[kbuf_free[0]], inc=cV)
        plan.emit("vector", lambda e: e.tensor_copy(k_buf[1][64:96, :], k_rf[0:32, :]),
                  waits=[kbuf_free[1]], inc=cV)
        plan.emit("vector", lambda e: e.memset(
            v_aug[0][:].rearrange("p (c ww) -> p c ww", ww=65)[:, :, 64:65], 1.0),
            waits=[vaug_free[0]], inc=cV)
        plan.emit("vector", lambda e: e.memset(
            v_aug[1][:].rearrange("p (c ww) -> p c ww", ww=65)[:, :, 64:65], 1.0),
            waits=[vaug_free[1]], inc=cV)
        for cc in range(NT):
            bi = cc % 2
            for kk in range(4):
                plan.emit("tensor", lambda e, hp=hp, cc=cc, bi=bi, kk=kk: e.matmul(
                    ps_kv[bi][:, 0:128], ckv_v[:, kk, 128 * cc:128 * cc + 128],
                    wuv[:, 1024 * kk + 128 * hp:1024 * kk + 128 * hp + 128],
                    start=(kk == 0), stop=(kk == 3)),
                    waits=([pskv_free[bi][0]] if kk == 0 else []),
                    inc=(cP if kk == 3 else None))
            mm_done = cP.n
            plan.emit("vector", lambda e, cc=cc, bi=bi: e.tensor_copy(
                v_aug[0][:, 65 * cc:65 * cc + 64], ps_kv[bi][:, 0:64]),
                waits=[wv(cP, mm_done)], inc=cV)
            plan.emit("vector", lambda e, cc=cc, bi=bi: e.tensor_copy(
                v_aug[1][:, 65 * cc:65 * cc + 64], ps_kv[bi][:, 64:128]),
                inc=cV)
            pskv_free[bi][0] = w(cV)

    bcmm_done = [0]

    def attn_pair(h0, h1):
        kv_ready = (w(cV), w(cA))
        quads = [(h, g, qd) for h in (h0, h1) for g in range(8) for qd in range(g + 1)]
        pv_q = []
        tail_q = []

        def emit_pv(h, g, qd, par, last_quad, m_done):
            p = h % 2
            vb = v_aug[p]
            at = ps_at2[g % 2]
            for j in range(4):
                cc = 4 * qd + j
                plan.emit("tensor", lambda e, par=par, cc=cc, j=j, vb=vb, qd=qd, at=at,
                          last_quad=last_quad: e.matmul(
                    at[0:65, 0:256], vb[:, 65 * cc:65 * cc + 65],
                    pt_buf[par][:, 256 * j:256 * j + 256],
                    start=(qd == 0 and j == 0), stop=(last_quad and j == 3),
                    skip_group_check=True),
                    waits=([wv(cV, m_done)] + ([psat_free[g % 2][0]]
                                               if (qd == 0 and j == 0)
                                               else [])) if j == 0 else [],
                    inc=(cP if j == 3 else None))
            pt_free[par] = w(cP)
            if last_quad:
                while tail_q:
                    tail_q.pop(0)[1]()
                tail_q.append((cur_i[0], make_tail(h, g)))

        def make_tail(h, g):
            pv_done = cP.n
            at = ps_at2[g % 2]
            plan.emit("scalar", lambda e, at=at: e.activation(
                out=rsum[0:1, 0:256], in_=at[64:65, 0:256], func=AF.Ln),
                waits=[wv(cP, pv_done), w(cA)], inc=cA)
            plan.emit("scalar", lambda e: e.activation(
                out=rsum_bf[0:1, 0:256], in_=rsum[0:1, 0:256], func=AF.Exp, scale=-1.0),
                waits=[w(cA), wv(cP, bcmm_done[0])], inc=cA)
            rs_done = cA.n

            def tail():
                plan.emit("tensor", lambda e: e.matmul(
                    ps_bc[0:64, 0:256], ones64[0:1, 0:64], rsum_bf[0:1, 0:256],
                    start=True, stop=True),
                    waits=[wv(cA, rs_done), psbc_free[0]], inc=cP)
                bcmm_done[0] = cP.n
                bc_done = cP.n
                plan.emit("vector", lambda e: e.tensor_copy(
                    bc_sb[0:64, 0:256], ps_bc[0:64, 0:256]),
                    waits=[wv(cP, bc_done)], inc=cV)
                psbc_free[0] = w(cV)
                bcs_done = cV.n
                plan.emit("vector", lambda e, h=h, g=g, at=at: e.tensor_tensor(
                    out=attn[64 * (h % 2):64 * (h % 2) + 64,
                             TOK * (h // 2) + 256 * g:TOK * (h // 2) + 256 * g + 256],
                    in0=at[0:64, 0:256], in1=bc_sb[0:64, 0:256], op=ALU.mult),
                    waits=[wv(cV, bcs_done), w(cV)], inc=cV)
                psat_free[g % 2][0] = w(cV)
            return tail

        cur_i = [0]
        for i, (h, g, qd) in enumerate(quads):
            cur_i[0] = i
            p = h % 2
            kb = k_buf[p]
            par = i % 2
            for j in range(4):
                cc = 4 * qd + j
                plan.emit("tensor", lambda e, h=h, g=g, cc=cc, j=j, par=par,
                          kb=kb: e.matmul(
                    ps_s[par][:, 256 * j:256 * j + 256],
                    kb[0:96, 128 * cc:128 * cc + 128],
                    q_all[0:96, TOK * h + 256 * g:TOK * h + 256 * g + 256],
                    start=True, stop=True),
                    waits=(([pss_free[par][0]] +
                            (list(kv_ready) if i == 0 else []))
                           if j == 0 else []),
                    inc=(cP if j == 3 else None))
            s_done = cP.n
            if tail_q and (i - tail_q[0][0]) >= 2:
                tail_q.pop(0)[1]()
            if len(pv_q) >= 2:
                emit_pv(*pv_q.pop(0))
            plan.emit("scalar", lambda e, par=par: e.activation(
                out=pt_buf[par][:], in_=ps_s[par][:, 0:1024], func=AF.Exp,
                scale=SCALE),
                waits=[wv(cP, s_done), pt_free[par]], inc=cA)
            pss_free[par][0] = w(cA)
            e_done = cA.n
            last_quad = (qd == g)
            plan.emit("vector", lambda e, par=par: e.tensor_scalar(
                out=pt_buf[par][:], in0=pt_buf[par][:], scalar1=EHI, scalar2=ELO,
                op0=ALU.min, op1=ALU.max),
                waits=[wv(cA, e_done)], inc=cV)
            if last_quad:
                plan.emit("vector", lambda e, par=par: e.tensor_tensor(
                    out=pt_buf[par][:], in0=pt_buf[par][:],
                    in1=masks[:, 0:1024], op=ALU.mult),
                    waits=[w(cV)], inc=cV)
            m_done = cV.n
            pv_q.append((h, g, qd, par, last_quad, m_done))
        while pv_q:
            emit_pv(*pv_q.pop(0))
            if tail_q:
                tail_q.pop(0)[1]()
        while tail_q:
            tail_q.pop(0)[1]()
        for p in (0, 1):
            kbuf_free[p] = w(cP)
            vaug_free[p] = w(cP)

    for hp in range(8):
        build_pair(hp)
        attn_pair(2 * hp, 2 * hp + 1)
        if hp == 7:
            plan.emit("gpsimd", lambda e: e.dma_start(out=wo_sb[:], in_=d_wo),
                      waits=[w(cP)], inc=cG, amt=16)
    wo_ld = cG.n

    attn_done = cV.n
    attn_v = attn[:].rearrange("p (k n) -> p k n", k=8)
    acc_v = acc[:].rearrange("p (m n) -> p m n", m=8)
    pswo_free = [(cV, 0), (cV, 0)]
    for m in range(8):
        for nt in range(4):
            bi = (m * 4 + nt) % 2
            for kk in range(8):
                plan.emit("tensor", lambda e, m=m, nt=nt, bi=bi, kk=kk: e.matmul(
                    ps_s[bi][:, 0:512],
                    wo_sb[:, 1024 * kk + 128 * m:1024 * kk + 128 * m + 128],
                    attn_v[:, kk, 512 * nt:512 * nt + 512],
                    start=(kk == 0), stop=(kk == 7)),
                    waits=([wv(cG, wo_ld), wv(cV, attn_done), pswo_free[bi],
                            pss_free[bi][0]] if kk == 0 else []),
                    inc=(cP if kk == 7 else None))
            mm_done = cP.n
            plan.emit("vector", lambda e, m=m, nt=nt, bi=bi: e.tensor_copy(
                acc_v[:, m, 512 * nt:512 * nt + 512], ps_s[bi][:, 0:512]),
                waits=[wv(cP, mm_done)], inc=cV)
            pswo_free[bi] = w(cV)
            pss_free[bi][0] = w(cV)
    es_psA.close()
    maybe_tap("t_attn", attn[:])
    maybe_tap("t_wo", acc[:])

    p3_mms_done = cP.n
    plan.emit("gpsimd", lambda e: e.dma_start(out=xn_own[:], in_=d_xn_sav),
              waits=[wv(cP, p3_mms_done)], inc=cG, amt=16)
    xn_reload = cG.n

    barrier()


    es_psF = ExitStack()
    ps_u1 = [pst(f"ps_u1{i}", [128, 1024], F32, es_psF) for i in range(2)]
    ps_u3 = [pst(f"ps_u3{i}", [128, 1024], F32, es_psF) for i in range(2)]
    ps_r = ps_u1
    ps_bcf = ps_u3[0]

    hh_v = h_half[:].rearrange("p (k n) -> p k n", k=8)
    w1_free = [(cP, 0), (cP, 0)]
    w3_free = [(cP, 0), (cP, 0)]
    w2_free = [(cP, 0)]
    u1_free = [(cA, 0), (cA, 0)]
    u3_free = [(cV, 0), (cV, 0)]
    silu_free = [(cV, 0)]
    tmp_free = [(cV, 0)]
    bcast_free = [(cV, 0)]
    w2_ld = [0]
    w13_ld = {}

    def expert_sources(E):
        if E == 0:
            return (lambda mq: d_ws1[mq, :, :], lambda mq: d_ws3[mq, :, :],
                    lambda Hf: d_ws2[Hf, :, :], None)
        ee = E - 1
        return (lambda mq, ee=ee: d_we1[ee, mq, :, :],
                lambda mq, ee=ee: d_we3[ee, mq, :, :],
                lambda Hf, ee=ee: d_we2[ee, Hf, :, :], ee)

    def load_w2(E, Hf):
        _, _, s2, _ = expert_sources(E)
        plan.emit("gpsimd", lambda e, s2=s2, Hf=Hf: e.dma_start(
            out=w2h[:], in_=s2(Hf)), waits=[w2_free[0]], inc=cG, amt=16)
        w2_ld[0] = cG.n

    def load_w13(E, Hf, m):
        i = E * 16 + Hf * 8 + m
        bi = i % 2
        s1, s3, _, _ = expert_sources(E)
        mq = Hf * 8 + m
        plan.emit("gpsimd", lambda e, bi=bi, s1=s1, mq=mq: e.dma_start(
            out=w1c[bi][:], in_=s1(mq)), waits=[w1_free[bi]], inc=cG, amt=16)
        plan.emit("gpsimd", lambda e, bi=bi, s3=s3, mq=mq: e.dma_start(
            out=w3c[bi][:], in_=s3(mq)), waits=[w3_free[bi]], inc=cG, amt=16)
        w13_ld[(E, Hf, m)] = cG.n

    step_list = [(E, Hf) for E in range(9) for Hf in range(2)]
    m_list = [(E, Hf, m) for E, Hf in step_list for m in range(8)]
    load_w13(0, 0, 0)
    load_w13(0, 0, 1)
    bc_done = 0

    for idx, (E, Hf) in enumerate(step_list):
        ee = None if E == 0 else E - 1
        load_w2(E, Hf)
        if ee is not None and Hf == 0:
            for nt in range(4):
                plan.emit("tensor", lambda e, ee=ee, nt=nt: e.matmul(
                    ps_bcf[:, 0:512], sel8[0:8, 128 * ee:128 * ee + 128],
                    wT[0:8, 512 * nt:512 * nt + 512], start=True, stop=True,
                    skip_group_check=True),
                    waits=[u3_free[0]] + ([bcast_free[0]] if nt == 0 else []), inc=cP)
                bcmm = cP.n
                plan.emit("scalar", lambda e, nt=nt: e.copy(
                    out=bcast[:, 512 * nt:512 * nt + 512], in_=ps_bcf[:, 0:512]),
                    waits=[wv(cP, bcmm)], inc=cA)
                u3_free[0] = w(cA)
            bc_done = cA.n
        for m in range(8):
            mi = E * 16 + Hf * 8 + m
            wbi = mi % 2
            wld = w13_ld[(E, Hf, m)]
            for nh in range(2):
                pbi = (m * 2 + nh) % 2
                for kk in range(8):
                    for ns in range(2):
                        plan.emit("tensor", lambda e, kk=kk, ns=ns, wbi=wbi, pbi=pbi,
                                  nh=nh: e.matmul(
                            ps_u1[pbi][:, 512 * ns:512 * ns + 512],
                            w1c[wbi][:, 128 * kk:128 * kk + 128],
                            xo_v[:, kk, 1024 * nh + 512 * ns:1024 * nh + 512 * ns + 512],
                            start=(kk == 0), stop=(kk == 7), skip_group_check=True),
                            waits=([wv(cG, wld), wv(cG, xn_reload), u1_free[pbi]]
                                   if (kk == 0 and ns == 0) else []))
                for kk in range(8):
                    for ns in range(2):
                        last = (kk == 7 and ns == 1)
                        plan.emit("tensor", lambda e, kk=kk, ns=ns, wbi=wbi, pbi=pbi,
                                  nh=nh: e.matmul(
                            ps_u3[pbi][:, 512 * ns:512 * ns + 512],
                            w3c[wbi][:, 128 * kk:128 * kk + 128],
                            xo_v[:, kk, 1024 * nh + 512 * ns:1024 * nh + 512 * ns + 512],
                            start=(kk == 0), stop=(kk == 7), skip_group_check=True),
                            waits=([u3_free[pbi]] if (kk == 0 and ns == 0) else []),
                            inc=(cP if last else None))
                u_done = cP.n
                if nh == 1:
                    w1_free[wbi] = w(cP)
                    w3_free[wbi] = w(cP)
                plan.emit("scalar", lambda e, pbi=pbi: e.activation(
                    out=silu_t[:, 0:1024], in_=ps_u1[pbi][:, 0:1024], func=AF.Silu),
                    waits=[wv(cP, u_done), silu_free[0]], inc=cA)
                u1_free[pbi] = w(cA)
                sl_done = cA.n
                plan.emit("vector", lambda e, m=m, nh=nh, pbi=pbi: e.tensor_tensor(
                    out=hh_v[:, m, 1024 * nh:1024 * nh + 1024], in0=silu_t[:, 0:1024],
                    in1=ps_u3[pbi][:, 0:1024], op=ALU.mult),
                    waits=[wv(cA, sl_done), w(cP)], inc=cV)
                u3_free[pbi] = w(cV)
                silu_free[0] = w(cV)
            nxt = mi + 2
            if nxt < len(m_list):
                En, Hn, mn = m_list[nxt]
                load_w13(En, Hn, mn)
        h_done = cV.n
        if E == 1 and Hf == 0:
            maybe_tap("t_h", h_half[:])
            maybe_tap("t_bc", bcast[:])
        for m in range(8):
            for nt in range(4):
                pbi = (m * 4 + nt) % 2
                for kk in range(8):
                    plan.emit("tensor", lambda e, m=m, nt=nt, kk=kk, pbi=pbi: e.matmul(
                        ps_r[pbi][:, 0:512],
                        w2h[:, 1024 * kk + 128 * m:1024 * kk + 128 * m + 128],
                        hh_v[:, kk, 512 * nt:512 * nt + 512],
                        start=(kk == 0), stop=(kk == 7), skip_group_check=True),
                        waits=([wv(cG, w2_ld[0]), wv(cV, h_done), u1_free[pbi]]
                               if kk == 0 else []),
                        inc=(cP if kk == 7 else None))
                r_done = cP.n
                if E == 0:
                    plan.emit("vector", lambda e, m=m, nt=nt, pbi=pbi: e.scalar_tensor_tensor(
                        out=acc_v[:, m, 512 * nt:512 * nt + 512], in0=ps_r[pbi][:, 0:512],
                        scalar=1.0 / 3.0, in1=acc_v[:, m, 512 * nt:512 * nt + 512],
                        op0=ALU.mult, op1=ALU.add),
                        waits=[wv(cP, r_done)], inc=cV)
                    u1_free[pbi] = w(cV)
                else:
                    plan.emit("vector", lambda e, nt=nt, pbi=pbi: e.tensor_tensor(
                        out=tmp_f[:, 0:512], in0=ps_r[pbi][:, 0:512],
                        in1=bcast[:, 512 * nt:512 * nt + 512], op=ALU.mult),
                        waits=[wv(cP, r_done), wv(cA, bc_done), tmp_free[0]], inc=cV)
                    u1_free[pbi] = w(cV)
                    plan.emit("vector", lambda e, m=m, nt=nt: e.tensor_tensor(
                        out=acc_v[:, m, 512 * nt:512 * nt + 512], in0=tmp_f[:, 0:512],
                        in1=acc_v[:, m, 512 * nt:512 * nt + 512], op=ALU.add),
                        waits=[w(cV)], inc=cV)
                    tmp_free[0] = w(cV)
        if ee is not None:
            bcast_free[0] = w(cV)
        w2_free[0] = w(cP)
        if E == 0 and Hf == 1:
            maybe_tap("t_acc_sh", acc[:])
    es_psF.close()
    maybe_tap("t_acc", acc[:])

    barrier()

    es_p5 = ExitStack()
    ps_o = [pst(f"ps_o{i}", [128, 1024], F32, es_p5) for i in range(4)]
    xo_free2 = [(cV, 0), (cV, 0), (cV, 0), (cV, 0)]
    out_free = [(cS, 0), (cS, 0)]
    pso_free = [(cV, 0)] * 4
    for o in range(OWN):
        bi = o % 2
        b4 = o % 4
        plan.emit("sync", lambda e, o=o, b4=b4: e.dma_start(
            out=xo_buf[b4][:], in_=d_xo[128 * o:128 * o + 128, :]),
            waits=[xo_free2[b4]], inc=cS, amt=16)
        xld = cS.n
        for kk in range(8):
            plan.emit("tensor", lambda e, o=o, kk=kk, b4=b4: e.transpose(
                ps_o[b4][:, 128 * kk:128 * kk + 128],
                acc_v[:, kk, 128 * o:128 * o + 128], ident_f32[:]),
                waits=([pso_free[b4]] if kk == 0 else []), inc=(cP if kk == 7 else None))
        tr_done = cP.n
        plan.emit("vector", lambda e, bi=bi, b4=b4: e.tensor_add(
            out_sb[bi][:], ps_o[b4][:, 0:1024], xo_buf[b4][:]),
            waits=[wv(cP, tr_done), wv(cS, xld), out_free[bi]], inc=cV)
        pso_free[b4] = w(cV)
        xo_free2[b4] = w(cV)
        add_done = cV.n
        plan.emit("gpsimd", lambda e, o=o, bi=bi: e.dma_start(
            out=d_out[128 * o:128 * o + 128, :], in_=out_sb[bi][:]),
            waits=[wv(cV, add_done)], inc=cG, amt=16)
        out_free[bi] = w(cG)
    es_p5.close()

    final_out = cS.n
    final_out_g = cG.n

    def _final_wait(eng, v=final_out, vg=final_out_g):
        eng.wait_ge(cS.sem, v)
        eng.wait_ge(cG.sem, vg)
    plan.ops["sync"].append(_final_wait)

    with es_root, nc.Block() as block:
        @block.sync
        def _(eng):
            for do in plan.ops["sync"]:
                do(eng)

        @block.tensor
        def _(eng):
            for do in plan.ops["tensor"]:
                do(eng)

        @block.vector
        def _(eng):
            for do in plan.ops["vector"]:
                do(eng)

        @block.scalar
        def _(eng):
            for do in plan.ops["scalar"]:
                do(eng)

        @block.gpsimd
        def _(eng):
            for do in plan.ops["gpsimd"]:
                do(eng)

    return nc



def _pack_lhsT(wt):
    K, M = wt.shape
    return np.ascontiguousarray(
        wt.reshape(K // 128, 128, M).transpose(1, 0, 2).reshape(128, -1))


def _pack_cols(wmat, ncol=128):
    K, Mt = wmat.shape
    out = np.empty((Mt // ncol, 128, (K // 128) * ncol), np.float32)
    for mq in range(Mt // ncol):
        out[mq] = _pack_lhsT(wmat[:, ncol * mq:ncol * mq + ncol])
    return np.ascontiguousarray(out)


def _rope_tables(positions, d=DR):
    inv_freq = (10000.0 ** (-np.arange(0, d, 2, dtype=np.float32) / d))
    ang = positions.astype(np.float32)[:, None] * inv_freq[None, :]
    return np.cos(ang).astype(np.float32), np.sin(ang).astype(np.float32)


def _tile_pack(tab, w):
    n = tab.shape[0] // 128
    return np.ascontiguousarray(
        tab.reshape(n, 128, w).transpose(1, 0, 2).reshape(128, -1))


def _core_masks(e):
    import ml_dtypes
    tri = (np.arange(128)[:, None] <= np.arange(128)[None, :]).astype(np.float32)
    one = np.ones((128, 128), np.float32)
    zero = np.zeros((128, 128), np.float32)

    def half(j, t_off):
        return one if j < t_off else (tri if j == t_off else zero)
    ms = [np.concatenate([half(j, e), half(j, 2 + e)], axis=1) for j in range(4)]
    full = np.concatenate([ms[0], ms[1], ms[2], ms[3]], axis=1)
    return full.astype(ml_dtypes.bfloat16)


_NC_CACHE = {}


def prepare_in_maps(inputs):
    import ml_dtypes
    x = np.ascontiguousarray(np.asarray(inputs["L0_attn_x"], np.float32))
    W = {k: np.ascontiguousarray(np.asarray(v, np.float32)) for k, v in inputs.items()
         if k != "L0_attn_x"}

    shared = {
        "wdqT": _pack_lhsT(W["L0_attn_W_DQ"].T),
        "wdkvT": _pack_lhsT(W["L0_attn_W_DKV"].T),
        "wuqT": _pack_lhsT(W["L0_attn_W_UQ"].T),
        "wqrT": _pack_lhsT(W["L0_attn_W_QR"].T),
        "wukT": _pack_lhsT(W["L0_attn_W_UK"].T),
        "wuvT": _pack_lhsT(W["L0_attn_W_UV"].T),
        "wkrT": _pack_lhsT(W["L0_attn_W_KR"].T),
        "woT": _pack_lhsT(W["L0_attn_W_O"].T),
        "d_wg": _pack_lhsT(W["L0_ffn_Wg"]),
        "ws1": _pack_cols(W["L0_ffn_Ws1"]),
        "ws3": _pack_cols(W["L0_ffn_Ws3"]),
        "ws2": np.stack([_pack_lhsT(W["L0_ffn_Ws2"][1024 * Hf:1024 * Hf + 1024, :])
                         for Hf in range(2)]),
        "we1": np.stack([_pack_cols(W["L0_ffn_We1"][ee]) for ee in range(8)]),
        "we3": np.stack([_pack_cols(W["L0_ffn_We3"][ee]) for ee in range(8)]),
        "we2": np.stack([np.stack(
            [_pack_lhsT(W["L0_ffn_We2"][ee][1024 * Hf:1024 * Hf + 1024, :])
             for Hf in range(2)]) for ee in range(8)]),
        "d_ident_bf": np.eye(128, dtype=ml_dtypes.bfloat16),
        "d_ident_f32": np.eye(128, dtype=np.float32),
        "d_ones64": np.ones((1, 64), dtype=ml_dtypes.bfloat16),
    }
    sel8 = np.zeros((8, 1024), np.float32)
    for ee in range(8):
        sel8[ee, 128 * ee:128 * ee + 128] = 1.0
    shared["d_sel8"] = sel8.astype(ml_dtypes.bfloat16)
    ckc, cks = _rope_tables(np.arange(S))
    shared["ropek_cos"] = _tile_pack(ckc, 16)
    shared["ropek_sin"] = _tile_pack(cks, 16)

    in_maps = []
    metas = []
    for c in range(8):
        b, e = c // 2, c % 2
        own_rows = np.concatenate(
            [np.arange(128 * (2 * s + e), 128 * (2 * s + e) + 128) for s in range(OWN)])
        qc, qs = _rope_tables(own_rows)
        qc16 = np.tile(qc, (1, 16)).astype(np.float32)
        qs16 = np.tile(qs, (1, 16)).astype(np.float32)
        m = dict(shared)
        m["x_seq"] = np.ascontiguousarray(x[b])
        m["x_own"] = np.ascontiguousarray(x[b][own_rows])
        m["d_masks"] = np.asarray(_core_masks(e))
        m["ropeq_cos"] = _tile_pack(qc16, 256)
        m["ropeq_sin"] = _tile_pack(qs16, 256)
        in_maps.append(m)
        metas.append((b, own_rows))
    return in_maps, metas


def kernel(**inputs):
    in_maps, metas = prepare_in_maps(inputs)
    if "prog" not in _NC_CACHE:
        _NC_CACHE["prog"] = build_program()
    nc = _NC_CACHE["prog"]
    res = run_bass_kernel_spmd(nc, in_maps, core_ids=list(range(8)))

    out = np.empty((B, S, D), np.float32)
    for c in range(8):
        b, own_rows = metas[c]
        out[b][own_rows, :] = res.results[c]["out"]
    return out
